# version 13
# baseline (speedup 1.0000x reference)
import numpy as np
from contextlib import ExitStack
import concourse.bass as bass
import concourse.mybir as mybir
from concourse.bass_utils import run_bass_kernel_spmd

F32 = mybir.dt.float32
BF16 = mybir.dt.bfloat16
AX = mybir.AxisListType.X
ALU = mybir.AluOpType
AF = mybir.ActivationFunctionType

S, D, H, HD = 4096, 1024, 16, 64
DC = D // 128
NT = S // 128
EPS = 1e-5
NCORES = 8

ENGINES = ("sync", "tensor", "vector", "scalar", "gpsimd")
DBG_SPILL_EXT = False
DMA_SEMS = ["lc", "lx0", "lx1", "lr0", "lw0", "lw1", "lg0", "lg1",
            "lk0", "lk1", "lq0", "lq1", "sk0", "sk1", "sx0", "sx1", "so", "gp"]


class Sch:

    def __init__(self, sem_names):
        self.items = {e: [] for e in ENGINES}
        self.cnt = {s: 0 for s in sem_names}
        self.sems = {}

    def put(self, eng, fn, waits=(), inc=None, amt=1):
        tgt = None
        if inc is not None:
            self.cnt[inc] += amt
            tgt = self.cnt[inc]
        ws = tuple((s, v) for (s, v) in waits if v is not None and v > 0)
        self.items[eng].append((ws, fn, inc, amt))
        return tgt

    def fence(self):
        return dict(self.cnt)

    def wf(self, f):
        return [(s, v) for s, v in f.items() if v > 0]

    def run(self, eng_name, eng):
        seen = {}
        for waits, fn, inc, amt in self.items[eng_name]:
            for s, v in waits:
                if seen.get(s, 0) >= v:
                    continue
                eng.wait_ge(self.sems[s], v)
                seen[s] = v
            ins = fn(eng)
            if inc is not None and ins is not None:
                ins.then_inc(self.sems[inc], amt)


class B:
    pass


SM_BN, SM_AG, SM_LNV, SM_RSTD = 0, 12, 14, 15
SM_M1, SM_M2, SM_DM, SM_S1, SM_S2 = 16, 17, 18, 19, 20
SM_LG, SM_EQ1, SM_MSK, SM_EQ2, SM_GAT = 24, 32, 40, 48, 56
SM_BQC, SM_BKC = 64, 72
SM_LGS, SM_M1S, SM_M2S, SM_S2S = 80, 88, 89, 90
SM_T2, SM_EQ1S, SM_EQ2S, SM_MSKS, SM_T2S, SM_GATS = 96, 104, 112, 120, 128, 136

BB_COLS = 57344
A_VAUG = 0
A_KBUF = 33280
A_QBUF0 = 37376
A_QBUF1 = 41472
A_RXQ = 45568
A_ATT = 45568
A_PT0 = 49664
A_PT1 = 51712
SP_WR, SP_WI, SP_WNEG, SP_FR, SP_FI = 0, 8192, 16384, 24576, 32768
SP_AT = 40960
SP_GR, SP_GIN = 0, 8192
SP_WEFF = 49152


def alloc_tensors(nc, ctx):
    b = B()
    b.nc = nc
    sb = lambda n, s, d: ctx.enter_context(nc.sbuf_tensor(n, s, d))
    psa = lambda n, s, d: ctx.enter_context(nc.psum_tensor(n, s, d))
    b.big = sb("big_bf", [128, BB_COLS], BF16)
    b.wbuf = sb("wbuf", [128, 8192], BF16)
    b.u_h = sb("u_h", [128, 1024], BF16)
    b.u2T_h = sb("u2T_h", [128, 1024], BF16)
    b.kstg = [sb("kstg0", [128, 512], BF16), sb("kstg1", [128, 512], BF16)]
    b.identb = sb("identb", [128, 128], BF16)
    b.mb_a = [sb("mb_a0", [128, 1024], BF16), sb("mb_a1", [128, 1024], BF16)]
    b.mb_b = [sb("mb_b0", [128, 512], BF16), sb("mb_b1", [128, 512], BF16)]
    b.xa = [sb("xa0", [128, 1024], F32), sb("xa1", [128, 1024], F32)]
    b.u_f = sb("u_f", [128, 1024], F32)
    b.zb = sb("zb", [128, 1024], F32)
    b.xn = [sb("xn0", [128, 1024], F32), sb("xn1", [128, 1024], F32)]
    b.tmpa = sb("tmpa", [128, 512], F32)
    b.tmpb = sb("tmpb", [128, 512], F32)
    b.rc = sb("rc", [128, 512], F32)
    b.rs = sb("rs", [128, 512], F32)
    b.bvb = sb("bvb", [128, 1024], F32)
    b.u2T_f = sb("u2T_f", [128, 1024], F32)
    b.stg_f = [sb("stg_f0", [128, 1024], F32), sb("stg_f1", [128, 1024], F32)]
    b.stg_g = [sb("stg_g0", [128, 512], F32), sb("stg_g1", [128, 512], F32)]
    b.rows = sb("rows", [128, 2048], F32)
    b.small = sb("small", [128, 160], F32)
    b.g_sb = sb("g_sb", [128, 64], F32)
    b.bc_sb = sb("bc_sb", [128, 512], F32)
    b.att_sc = sb("att_sc", [128, 1024], F32)
    b.tmpc = b.att_sc[:, 0:512]
    b.tmpd = b.att_sc[:, 512:1024]
    b.identf = sb("identf", [128, 128], F32)
    b.epsc = sb("epsc", [128, 1], F32)
    b.psA = psa("psA", [128, 1024], F32)
    b.psB = psa("psB", [128, 1024], F32)
    b.psC = psa("psC", [128, 1024], F32)
    b.ps_lg = psa("ps_lg", [128, 512], F32)
    b.pstr = psa("pstr", [128, 1024], BF16)
    b.onesf = b.rows[0:1, 0:512]
    b.stgf_free = [0, 0]
    b.stgg_free = [0, 0]
    b.ident_ld = 0
    b.rcp = b.rows[0:1, 512:1024]
    b.bias_row = b.rows[0:1, 1024:2048]
    return b


def sm(b, off, n=1):
    return b.small[:, off:off + n]


def emit_consts(b, names):
    s = b.sch
    b.ident_ld = s.put("sync", lambda e, o=b.identb[:, :], i=names["c_identb"][:, :]:
                       e.dma_start(out=o, in_=i), inc="lc", amt=16)
    if "c_identf" in names:
        b.ident_ld = s.put("sync", lambda e, o=b.identf[:, :], i=names["c_identf"][:, :]:
                           e.dma_start(out=o, in_=i), inc="lc", amt=16)
    s.put("vector", lambda e, o=b.onesf: e.memset(o, 1.0), inc="ve")
    s.put("vector", lambda e, o=b.epsc[:, :]: e.memset(o, EPS), inc="ve")
    b.nc.const_aps.register(F32, EPS, b.epsc[:, 0:1])


def emit_ln(b, src_ap, waits):
    s = b.sch
    st = b.ln_state
    s.put("vector", lambda e, o=sm(b, SM_BN, 6), i=src_ap[:, 0:512]: e.bn_stats(o, i),
          waits=list(waits) + [("ve", st.get("aggr_done", 0))])
    v1 = s.put("vector", lambda e, o=sm(b, SM_BN + 6, 6), i=src_ap[:, 512:1024]:
               e.bn_stats(o, i), inc="ve")
    v2 = s.put("vector", lambda e, o=sm(b, SM_AG, 2), i=sm(b, SM_BN, 12):
               e.bn_aggr(o, i),
               waits=[("se", st.get("ln_read", 0)), ("ve", v1),
                      ("ve", st.get("rstd_read", 0))], inc="ve")
    st["aggr_done"] = v2
    s1 = s.put("scalar", lambda e, o=sm(b, SM_LNV), i=sm(b, SM_AG + 1):
               e.activation(out=o, in_=i, func=AF.Ln, bias=EPS),
               waits=[("ve", v2)], inc="se")
    st["ln_read"] = s1
    s2 = s.put("scalar", lambda e, o=sm(b, SM_RSTD), i=sm(b, SM_LNV):
               e.activation(out=o, in_=i, func=AF.Exp, scale=-0.5),
               waits=[("se", s1), ("ve", st.get("rstd_read", 0))], inc="se")
    return sm(b, SM_AG), sm(b, SM_RSTD), s2


def emit_bias_bcast(b, dst, row_ap, extra_waits=()):
    s = b.sch
    w = list(extra_waits)
    s.put("tensor", lambda e, o=b.psC[:, 0:512], a=b.onesf[0:1, 0:128], r=row_ap[0:1, 0:512]:
          e.matmul(o, a, r, start=True, stop=True), waits=w)
    p = s.put("tensor", lambda e, o=b.psC[:, 512:1024], a=b.onesf[0:1, 0:128], r=row_ap[0:1, 512:1024]:
              e.matmul(o, a, r, start=True, stop=True), inc="pe")
    v = s.put("vector", lambda e, o=dst[:, :], i=b.psC[:, :]: e.tensor_copy(o, i),
              waits=[("pe", p)], inc="ve")
    return v


def emit_attn(b, names, L, xin, xout):
    s = b.sch
    nm = lambda k: names[f"a{L}_{k}"]
    qspill, kspill = names["qspill"], names["kspill"]
    rope = (L == 0)

    s.put("sync", lambda e, o=sm(b, SM_BQC, 8), i=nm("bqc")[:, :]: e.dma_start(out=o, in_=i), inc="lc", amt=16)
    s.put("sync", lambda e, o=sm(b, SM_BKC, 8), i=nm("bkc")[:, :]: e.dma_start(out=o, in_=i), inc="lc", amt=16)
    s.put("sync", lambda e, o=b.bias_row, i=nm("bv")[:, :]: e.dma_start(out=o, in_=i), inc="lc", amt=16)
    f0 = s.fence()
    emit_bias_bcast(b, b.bvb, b.bias_row, extra_waits=s.wf(f0))
    vmemset = s.put("vector", lambda e, o=b.big[:, A_VAUG:A_VAUG + 33280]: e.memset(o, 1.0),
                    waits=s.wf(f0), inc="ve")

    xa_free = [0, 0]
    rope_free = 0
    pstr_free = 0
    prev_trv = 0
    bank_free = [0, 0, 0, 0]
    kstg_free = [0, 0]
    b.ln_state = {}

    for Q in range(4):
        rxq_done = None
        for t in range(8):
            g = Q * 8 + t
            sl = g % 2
            xr = s.put("sync", lambda e, o=b.xa[sl][:, :], i=xin[g * 128:(g + 1) * 128, :]:
                       e.dma_start(out=o, in_=i), waits=[("ve", xa_free[sl])],
                       inc=f"lx{sl}", amt=16)
            if rope:
                s.put("sync", lambda e, o=b.rc[:, :], i=names["rcos"][g * 128:(g + 1) * 128, :]:
                      e.dma_start(out=o, in_=i), waits=[("ve", rope_free)], inc="lr0", amt=16)
                rr = s.put("sync", lambda e, o=b.rs[:, :], i=names["rsin"][g * 128:(g + 1) * 128, :]:
                           e.dma_start(out=o, in_=i), inc="lr0", amt=16)
            mean, rstd, sd = emit_ln(b, b.xa[sl], waits=[(f"lx{sl}", xr)])
            if DBG_SPILL_EXT and "dbg_ag" in names and L == 0:
                s.put("gpsimd", lambda e, o=names["dbg_ag"][g * 128:(g + 1) * 128, :],
                      i=sm(b, SM_AG, 4): e.dma_start(out=o, in_=i),
                      waits=[("se", sd)], inc="so", amt=16)
            uv = s.put("vector", lambda e, o=b.u_f[:, :], i=b.xa[sl][:, :], m=mean, r=rstd:
                       e.tensor_scalar(o, i, m, r, ALU.subtract, ALU.mult),
                       waits=[("se", sd), ("ve", rope_free)], inc="ve")
            b.ln_state["rstd_read"] = uv
            xa_free[sl] = uv
            if DBG_SPILL_EXT and "dbg_uf" in names and L == 0:
                s.put("gpsimd", lambda e, o=names["dbg_uf"][g * 128:(g + 1) * 128, :],
                      i=b.u_f[:, :]: e.dma_start(out=o, in_=i),
                      waits=[("ve", uv)], inc="so", amt=16)
            if rope:
                g1 = s.put("gpsimd", lambda e, o=b.tmpa[:, :], i=b.u_f[:, 0:512], j=b.rc[:, :]:
                           e.tensor_mul(o, i, j),
                           waits=[("lr0", rr), ("pe", prev_trv), ("ve", rope_free), ("ve", uv)])
                g2 = s.put("gpsimd", lambda e, o=b.tmpb[:, :], i=b.u_f[:, 512:1024], j=b.rs[:, :]:
                           e.tensor_mul(o, i, j), inc="gp")
                g3 = s.put("gpsimd", lambda e, o=b.tmpc[:, :], i=b.u_f[:, 0:512], j=b.rs[:, :]:
                           e.tensor_mul(o, i, j))
                g4 = s.put("gpsimd", lambda e, o=b.tmpd[:, :], i=b.u_f[:, 512:1024], j=b.rc[:, :]:
                           e.tensor_mul(o, i, j), inc="gp")
                r3 = s.put("vector", lambda e, o=b.u_h[:, 0:512], i=b.tmpa[:, :], j=b.tmpb[:, :]:
                           e.tensor_sub(o, i, j), waits=[("gp", g2)], inc="ve")
                uh = s.put("vector", lambda e, o=b.u_h[:, 512:1024], i=b.tmpc[:, :], j=b.tmpd[:, :]:
                           e.tensor_add(o, i, j), waits=[("gp", g4)], inc="ve")
                rope_free = uh
            else:
                uh = s.put("vector", lambda e, o=b.u_h[:, :], i=b.u_f[:, :]: e.tensor_copy(o, i),
                           waits=[("pe", prev_trv), ("ve", uv)], inc="ve")
            for c in range(8):
                w = [("ve", uh)] + ([("ve", pstr_free), ("lc", b.ident_ld)] if c == 0 else [])
                kw = dict(inc="pe") if c == 7 else {}
                tr = s.put("tensor", lambda e, o=b.pstr[:, c * 128:(c + 1) * 128],
                           i=b.u_h[:, c * 128:(c + 1) * 128], d=b.identb[:, :]:
                           e.transpose(o, i, d), waits=w, **kw)
            if DBG_SPILL_EXT and "dbg_uh" in names and L == 0:
                s.put("gpsimd", lambda e, o=names["dbg_uh"][g * 128:(g + 1) * 128, :],
                      i=b.u_h[:, :]: e.dma_start(out=o, in_=i),
                      waits=[("ve", uh)], inc="so", amt=16)
            s.put("tensor", lambda e, o=b.ps_lg[:, :], a=b.identb[:, :],
                  r=b.u_h[:, 0:512]: e.matmul(o, a, r, start=True, stop=True))
            dst = b.big[:, A_RXQ + t * 128: A_RXQ + t * 128 + 8 * 1024]
            dst = dst.rearrange("p (c x) -> p c x", x=1024)[:, :, 0:128]
            rxq_done = s.put("vector", lambda e, o=dst, i=b.pstr[:, :].rearrange("p (c x) -> p c x", x=128):
                             e.tensor_copy(o, i), waits=[("pe", tr)], inc="ve")
            pstr_free = rxq_done
            prev_trv = tr

        if DBG_SPILL_EXT and "dbg_rxq" in names and L == 0:
            s.put("gpsimd", lambda e, o=names["dbg_rxq"][Q * 128:(Q + 1) * 128, :],
                  i=b.big[:, A_RXQ:A_RXQ + 8192]: e.dma_start(out=o, in_=i),
                  waits=[("ve", rxq_done)], inc="so", amt=16)
        wbuf_free = b.wr_fence if hasattr(b, "wr_fence") else 0
        psum_half = [b.psA[:, 0:512], b.psA[:, 512:1024], b.psB[:, 0:512], b.psB[:, 512:1024]]

        for wsel in ("k", "v", "q"):
            wt = nm("w" + wsel + "T")
            for c in range(8):
                lr = s.put("sync", lambda e, o=b.stg_f[c % 2][:, :], i=wt[c * 128:(c + 1) * 128, :]:
                           e.dma_start(out=o, in_=i), waits=[("ve", b.stgf_free[c % 2])],
                           inc=f"lw{c % 2}", amt=16)
                b.stgf_free[c % 2] = s.put("vector", lambda e, o=b.wbuf[:, c * 1024:(c + 1) * 1024],
                                           i=b.stg_f[c % 2][:, :]: e.tensor_copy(o, i),
                                           waits=[(f"lw{c % 2}", lr), ("pe", wbuf_free)], inc="ve")
            cast_all = b.stgf_free[1]
            if DBG_SPILL_EXT and f"dbg_w{wsel}" in names and L == 0 and Q == 0:
                s.put("gpsimd", lambda e, o=names[f"dbg_w{wsel}"][:, :], i=b.wbuf[:, :]:
                      e.dma_start(out=o, in_=i), waits=[("ve", cast_all)], inc="so", amt=16)
            if wsel in ("k", "q"):
                spill = kspill if wsel == "k" else qspill
                bcol = SM_BQC if wsel == "q" else SM_BKC
                for tokN in range(2):
                    for dg in range(2):
                        grp_done = [None] * 4
                        for d in range(8):
                            for j in range(4):
                                w = []
                                if d == 0:
                                    w = [("ve", cast_all), ("ve", rxq_done), ("ve", bank_free[j])]
                                kw = dict(inc="pe") if d == 7 else {}
                                grp_done[j] = s.put("tensor", lambda e, o=psum_half[j],
                                                    a=b.wbuf[:, d * 1024 + (dg * 4 + j) * 128: d * 1024 + (dg * 4 + j) * 128 + 128],
                                                    r=b.big[:, A_RXQ + d * 1024 + tokN * 512: A_RXQ + d * 1024 + tokN * 512 + 512],
                                                    st=(d == 0), sp=(d == 7):
                                                    e.matmul(o, a, r, start=st, stop=sp), waits=w, **kw)
                        for j in range(4):
                            ch = dg * 4 + j
                            if wsel == "q":
                                ev = s.put("vector", lambda e, o=b.kstg[j % 2][:, :], i=psum_half[j],
                                           m=sm(b, bcol + ch):
                                           e.tensor_scalar(o, i, m, 0.125, ALU.add, ALU.mult),
                                           waits=[("pe", grp_done[j]), (f"sk{j % 2}", kstg_free[j % 2])], inc="ve")
                            else:
                                ev = s.put("vector", lambda e, o=b.kstg[j % 2][:, :], i=psum_half[j],
                                           m=sm(b, bcol + ch):
                                           e.tensor_scalar(o, i, m, None, ALU.add),
                                           waits=[("pe", grp_done[j]), (f"sk{j % 2}", kstg_free[j % 2])], inc="ve")
                            bank_free[j] = ev
                            kstg_free[j % 2] = s.put("gpsimd", lambda e, o=spill[ch * 128:(ch + 1) * 128, Q * 1024 + tokN * 512: Q * 1024 + tokN * 512 + 512],
                                                     i=b.kstg[j % 2][:, :]: e.dma_start(out=o, in_=i),
                                                     waits=[("ve", ev)], inc=f"sk{j % 2}", amt=16)
            else:
                for t in range(8):
                    g = Q * 8 + t
                    for dh in range(2):
                        j = (t * 2 + dh) % 4
                        grp = None
                        for d in range(8):
                            w = []
                            if d == 0:
                                w = [("ve", cast_all), ("ve", rxq_done), ("ve", bank_free[j]),
                                     ("ve", vmemset)]
                            kw = dict(inc="pe") if d == 7 else {}
                            grp = s.put("tensor", lambda e, o=psum_half[j],
                                        a=b.big[:, A_RXQ + d * 1024 + t * 128: A_RXQ + d * 1024 + t * 128 + 128],
                                        r=b.wbuf[:, d * 1024 + dh * 512: d * 1024 + dh * 512 + 512],
                                        st=(d == 0), sp=(d == 7):
                                        e.matmul(o, a, r, start=st, stop=sp), waits=w, **kw)
                        voff = A_VAUG + g * 1040 + dh * 8 * 65
                        dst = b.big[:, voff:voff + 520].rearrange("p (h x) -> p h x", x=65)[:, :, 0:64]
                        bank_free[j] = s.put("vector", lambda e, o=dst,
                                             i=psum_half[j].rearrange("p (h x) -> p h x", x=64),
                                             bb=b.bvb[:, dh * 512:dh * 512 + 512].rearrange("p (h x) -> p h x", x=64):
                                             e.tensor_tensor(o, i, bb, ALU.add),
                                             waits=[("pe", grp)], inc="ve")
            fw = s.fence()
            wbuf_free = fw["pe"]
        b.wr_fence = wbuf_free

    fA = s.fence()
    if DBG_SPILL_EXT and "dbg_vaug" in names and L == 0:
        s.put("gpsimd", lambda e, o=names["dbg_vaug"][:, :], i=b.big[:, A_VAUG:A_VAUG + 33280]:
              e.dma_start(out=o, in_=i), waits=s.wf(fA), inc="so", amt=16)
    s.put("sync", lambda e, o=b.bias_row, i=nm("bo")[:, :]: e.dma_start(out=o, in_=i),
          waits=[("pe", fA["pe"])], inc="lc", amt=16)
    fB0 = s.fence()
    bobv = emit_bias_bcast(b, b.bvb, b.bias_row, extra_waits=s.wf(fB0))
    for c in range(8):
        lw = [("ve", b.stgf_free[c % 2])]
        lr = s.put("sync", lambda e, o=b.stg_f[c % 2][:, :], i=nm("woT")[c * 128:(c + 1) * 128, :]:
                   e.dma_start(out=o, in_=i), waits=lw, inc=f"lw{c % 2}", amt=16)
        b.stgf_free[c % 2] = s.put("vector", lambda e, o=b.wbuf[:, c * 1024:(c + 1) * 1024],
                                   i=b.stg_f[c % 2][:, :]: e.tensor_copy(o, i),
                                   waits=[(f"lw{c % 2}", lr)] + ([("pe", fA["pe"])] if c == 0 else []),
                                   inc="ve")
    wo_ready = b.stgf_free[1]

    qdone = [fA["pe"], fA["pe"]]
    kb_free = [fA["pe"], fA["pe"]]
    qrdy = [0, 0]
    ps_se = [fA["se"], fA["se"]]
    ps_ve = [bobv, bobv]
    acc_ve = [bobv, bobv]
    pT_free = [fA["pe"], fA["pe"]]
    tail_free = [fA["ve"], fA["ve"]]
    tt_prev = fA["ve"]
    bcc_free = bobv
    rcp_se_free = fA["se"]
    rcp_free = fA["pe"]
    attT_pe_free = fA["pe"]
    acc_free = bobv
    xa_free = [max(xa_free[0], 0), max(xa_free[1], 0)]
    store_cnt_for_xn = [0, 0]

    for qb in range(8):
        qsl = qb % 2
        qoff = (A_QBUF0, A_QBUF1)[qsl]
        for c in range(8):
            w = ([("pe", qdone[qsl]), ("sk0", fA["sk0"]), ("sk1", fA["sk1"])]
                 if c == 0 else [])
            qrdy[qsl] = s.put("sync", lambda e, o=b.big[:, qoff + c * 512: qoff + (c + 1) * 512],
                              i=qspill[c * 128:(c + 1) * 128, qb * 512:(qb + 1) * 512]:
                              e.dma_start(out=o, in_=i), waits=w, inc=f"lq{qsl}", amt=16)
        for h in range(16):
            hp = h % 2
            krdy = s.put("sync", lambda e,
                         o=b.big[hp * 64:hp * 64 + 64, A_KBUF:A_KBUF + 4096],
                         i=kspill[h * 64:(h + 1) * 64, :]:
                         e.dma_start(out=o, in_=i),
                         waits=[("pe", kb_free[hp]), ("sk0", fA["sk0"]), ("sk1", fA["sk1"])],
                         inc=f"lk{hp}", amt=16)
            kb = b.big[hp * 64:hp * 64 + 64, A_KBUF:A_KBUF + 4096]
            qch = b.big[hp * 64:hp * 64 + 64, qoff + (h // 2) * 512: qoff + (h // 2) * 512 + 512]
            pts = [b.big[:, A_PT0:A_PT0 + 1024], b.big[:, A_PT1:A_PT1 + 1024]]
            psX = [b.psA[:, :], b.psB[:, :]]
            acc = b.psC[:, 0:512]
            sd = {}
            av = {}
            ex = {}

            def sfill(u, first=False):
                sl2 = u % 2
                for c2 in range(2):
                    kt = 2 * u + c2
                    w0 = []
                    if c2 == 0:
                        w0 = [("se", ex.get(u - 2, ps_se[sl2])), ("ve", ps_ve[sl2])]
                        if first:
                            w0 += [(f"lk{hp}", krdy), (f"lq{qsl}", qrdy[qsl])]
                    kw = dict(inc="pe") if c2 == 1 else {}
                    last = s.put("tensor", lambda e, o=psX[sl2][:, c2 * 512:(c2 + 1) * 512],
                                 a=kb[:, kt * 128:kt * 128 + 128], r=qch:
                                 e.matmul(o, a, r, start=True, stop=True), waits=w0, **kw)
                    if c2 == 1:
                        sd[u] = last

            def avrun(u):
                sl2 = u % 2
                for c2 in range(2):
                    kt = 2 * u + c2
                    w0 = [("se", ex[u])] + ([("ve", acc_ve[0])] if u == 0 else []) if c2 == 0 else []
                    kw = dict(inc="pe") if c2 == 1 else {}
                    r_ = s.put("tensor", lambda e, o=acc[0:65, :],
                               a=b.big[:, A_VAUG + kt * 1040 + h * 65: A_VAUG + kt * 1040 + h * 65 + 65],
                               r=pts[sl2][:, c2 * 512:(c2 + 1) * 512],
                               st=(u == 0 and c2 == 0), sp=(u == 15 and c2 == 1):
                               e.matmul(o, a, r, start=st, stop=sp), waits=w0, **kw)
                    if c2 == 1:
                        av[u] = r_

            sfill(0, first=True)
            sfill(1, first=True)
            for u in range(16):
                ex[u] = s.put("scalar", lambda e, o=pts[u % 2], i=psX[u % 2][:, :]:
                              e.activation(out=o, in_=i, func=AF.Exp),
                              waits=[("pe", sd[u]), ("pe", av.get(u - 2, pT_free[u % 2]))],
                              inc="se")
                avrun(u)
                if u + 2 < 16:
                    sfill(u + 2)
                s.put("tensor", lambda e, o=b.ps_lg[:, :], a=b.identb[:, :],
                      r=b.big[:, A_VAUG:A_VAUG + 512]:
                      e.matmul(o, a, r, start=True, stop=True))
            kb_free[hp] = sd[15]
            ps_se[0] = ex[14]
            ps_se[1] = ex[15]
            pT_free[0] = av[14]
            pT_free[1] = av[15]
            slT = hp
            dA = s.put("vector", lambda e, o=b.att_sc[0:65, slT * 512:slT * 512 + 512],
                       i=acc[0:65, :]:
                       e.tensor_copy(o, i), waits=[("pe", av[15]), ("ve", tail_free[slT])],
                       inc="ve")
            acc_ve[0] = dA
            sLn = s.put("scalar", lambda e, o=b.bc_sb[64:65, :],
                        i=b.att_sc[64:65, slT * 512:slT * 512 + 512]:
                        e.activation(out=o, in_=i, func=AF.Ln),
                        waits=[("ve", dA), ("se", rcp_se_free)], inc="se")
            sRc = s.put("scalar", lambda e, o=b.rcp, i=b.bc_sb[64:65, :]:
                        e.activation(out=o, in_=i, func=AF.Exp, scale=-1.0),
                        waits=[("se", sLn), ("pe", rcp_free)], inc="se")
            rcp_se_free = sRc
            bcd = s.put("tensor", lambda e, o=b.psC[0:64, 512:1024], a=b.onesf[0:1, 0:64], r=b.rcp:
                        e.matmul(o, a, r, start=True, stop=True),
                        waits=[("se", sRc), ("ve", bcc_free)], inc="pe")
            rcp_free = bcd
            cC = s.put("vector", lambda e, o=b.bc_sb[0:64, :], i=b.psC[0:64, 512:1024]:
                       e.tensor_copy(o, i), waits=[("pe", bcd), ("ve", tt_prev)], inc="ve")
            bcc_free = cC
            attv = b.big[hp * 64:hp * 64 + 64, A_ATT + (h // 2) * 512: A_ATT + (h // 2) * 512 + 512]
            tt = s.put("vector", lambda e, o=attv,
                       i=b.att_sc[0:64, slT * 512:slT * 512 + 512], j=b.bc_sb[0:64, :]:
                       e.tensor_tensor(o, i, j, ALU.mult),
                       waits=[("ve", cC), ("pe", attT_pe_free)], inc="ve")
            tt_prev = tt
            tail_free[slT] = tt
            acc_free = tt
        attT_ready = acc_free
        qdone[qsl] = sd[15]

        prev_ev = 0
        prev_ap1 = 0
        for t in range(4):
            g = qb * 4 + t
            sl = t % 2
            xr = s.put("sync", lambda e, o=b.xa[sl][:, :], i=xin[g * 128:(g + 1) * 128, :]:
                       e.dma_start(out=o, in_=i), waits=[("ve", xa_free[sl])],
                       inc=f"lx{sl}", amt=16)
            po = [b.psC[:, 0:512], b.psC[:, 512:1024]]
            pod = [None, None]
            for dh in range(2):
                for d in range(8):
                    w = []
                    if dh == 0 and d == 0:
                        w = [("ve", attT_ready), ("se", ps_se[0]), ("se", ps_se[1]),
                             ("ve", wo_ready), ("ve", prev_ev)]
                    kw = dict(inc="pe") if d == 7 else {}
                    pod[dh] = s.put("tensor", lambda e, o=po[dh],
                                    a=b.big[:, A_ATT + d * 512 + t * 128: A_ATT + d * 512 + t * 128 + 128],
                                    r=b.wbuf[:, d * 1024 + dh * 512: d * 1024 + dh * 512 + 512],
                                    st=(d == 0), sp=(d == 7):
                                    e.matmul(o, a, r, start=st, stop=sp), waits=w, **kw)
            ev0 = s.put("vector", lambda e, o=b.zb[:, 0:512], i=po[0], j=b.bvb[:, 0:512]:
                        e.tensor_tensor(o, i, j, ALU.add),
                        waits=[("pe", pod[0]), ("ve", bobv), ("ve", prev_ap1)], inc="ve")
            ev1 = s.put("vector", lambda e, o=b.zb[:, 512:1024], i=po[1], j=b.bvb[:, 512:1024]:
                        e.tensor_tensor(o, i, j, ALU.add), waits=[("pe", pod[1])], inc="ve")
            prev_ev = ev1
            mean, rstd, sd = emit_ln(b, b.zb, waits=[("ve", ev1)])
            ap1 = s.put("vector", lambda e, o=b.u_f[:, :], i=b.zb[:, :], m=mean, r=rstd:
                        e.tensor_scalar(o, i, m, r, ALU.subtract, ALU.mult),
                        waits=[("se", sd), (f"lx{sl}", xr)], inc="ve")
            b.ln_state["rstd_read"] = ap1
            prev_ap1 = ap1
            xnv = s.put("vector", lambda e, o=b.xn[sl][:, :], i=b.u_f[:, :], j=b.xa[sl][:, :]:
                        e.tensor_add(o, i, j),
                        waits=[(f"sx{sl}", store_cnt_for_xn[sl]), ("ve", ap1)], inc="ve")
            xa_free[sl] = xnv
            store_cnt_for_xn[sl] = s.put("gpsimd", lambda e, o=xout[g * 128:(g + 1) * 128, :],
                                         i=b.xn[sl][:, :]: e.dma_start(out=o, in_=i),
                                         waits=[("ve", xnv)], inc=f"sx{sl}", amt=16)
        attT_pe_free = s.cnt["pe"]
        acc_ve[0] = s.cnt["ve"]
        acc_ve[1] = s.cnt["ve"]
    b.wr_fence = s.cnt["pe"]


def mb_wout_view(b, hk):
    if hk < 24:
        return b.big[:, 32768 + hk * 1024: 32768 + (hk + 1) * 1024]
    return b.wbuf[:, (hk - 24) * 1024:(hk - 23) * 1024]


def emit_mbuild(b, names, win, woutT, mout):
    s = b.sch
    psum_half = [b.psA[:, 0:512], b.psA[:, 512:1024], b.psB[:, 0:512], b.psB[:, 512:1024]]
    if not hasattr(b, "mb_state"):
        b.mb_state = dict(ev=[0, 0], st=[0, 0], mmdone=0)
    ms = b.mb_state
    ev_free = ms["ev"]
    st_free = ms["st"]
    cast_done = {}
    stgg_free = b.stgg_free
    for hk in range(32):
        sl = hk % 2
        lr = s.put("sync", lambda e, o=b.stg_f[sl][:, :], i=win(hk, False):
                   e.dma_start(out=o, in_=i), waits=[("ve", b.stgf_free[sl])],
                   inc=f"lw{sl}", amt=16)
        b.stgf_free[sl] = s.put("vector", lambda e, o=b.big[:, hk * 1024:(hk + 1) * 1024],
                                i=b.stg_f[sl][:, :]: e.tensor_copy(o, i),
                                waits=[(f"lw{sl}", lr), ("pe", ms["mmdone"])], inc="ve")
        lg0 = s.put("gpsimd", lambda e, o=b.stg_g[0][:, :], i=win(hk, True)[:, 0:512]:
                    e.dma_start(out=o, in_=i), waits=[("ve", stgg_free[0])],
                    inc="lg0", amt=16)
        lg1 = s.put("gpsimd", lambda e, o=b.stg_g[1][:, :], i=win(hk, True)[:, 512:1024]:
                    e.dma_start(out=o, in_=i), waits=[("ve", stgg_free[1])],
                    inc="lg1", amt=16)
        stgg_free[0] = s.put("vector", lambda e, o=mb_wout_view(b, hk)[:, 0:512],
                             i=b.stg_g[0][:, :]: e.tensor_copy(o, i),
                             waits=[("lg0", lg0), ("pe", ms["mmdone"])], inc="ve")
        stgg_free[1] = s.put("vector", lambda e, o=mb_wout_view(b, hk)[:, 512:1024],
                             i=b.stg_g[1][:, :]: e.tensor_copy(o, i),
                             waits=[("lg1", lg1), ("pe", ms["mmdone"])], inc="ve")
        cast_done[hk] = stgg_free[1]
    cast_last = cast_done[31]
    first_pass = True
    for dh in range(2):
        for dg in range(2):
            grp_done = [None] * 4
            for hk in range(32):
                for j in range(4):
                    w = []
                    if j == 0 and first_pass:
                        w = [("ve", cast_done[hk])]
                    if hk == 0 and j == 0:
                        w += [("ve", ev_free[0]), ("ve", ev_free[1])]
                        if not first_pass:
                            w += [("ve", cast_last)]
                    kw = dict(inc="pe") if (hk == 31 and j == 3) else {}
                    grp_done[j] = s.put("tensor", lambda e, o=psum_half[j],
                                        a=b.big[:, hk * 1024 + (dg * 4 + j) * 128: hk * 1024 + (dg * 4 + j) * 128 + 128],
                                        r=mb_wout_view(b, hk)[:, dh * 512:(dh + 1) * 512],
                                        st=(hk == 0), sp=(hk == 31):
                                        e.matmul(o, a, r, start=st, stop=sp), waits=w, **kw)
            first_pass = False
            pe_end = s.cnt["pe"]
            ms["mmdone"] = pe_end
            for j in range(4):
                ch = dg * 4 + j
                sl = j % 2
                ev = s.put("vector", lambda e, o=b.stg_g[sl][:, :], i=psum_half[j]:
                           e.tensor_copy(o, i), waits=[("pe", pe_end), (f"sk{sl}", st_free[sl])],
                           inc="ve")
                stgg_free[sl] = ev
                ev_free[sl] = ev
                st_free[sl] = s.put("gpsimd", lambda e,
                                    o=mout(ch, dh),
                                    i=b.stg_g[sl][:, :]: e.dma_start(out=o, in_=i),
                                    waits=[("ve", ev)], inc=f"sk{sl}", amt=16)


def m_view(b, e_, dc):
    idx = e_ * 8 + dc
    if idx < 56:
        return b.big[:, idx * 1024:(idx + 1) * 1024]
    return b.wbuf[:, (idx - 56) * 1024:(idx - 55) * 1024]


def emit_moe(b, names, xin, xout, m_src=None):
    s = b.sch
    if m_src is None:
        m_src = names["m_all"]
    gl = s.put("sync", lambda e, o=b.g_sb[:, :], i=names["g_wTc"][:, :]:
               e.dma_start(out=o, in_=i), inc="lc", amt=16)
    stg_free = [b.stgf_free[0], b.stgf_free[1]]
    for e_ in range(8):
        for c in range(8):
            sl = (e_ * 8 + c) % 2
            lr = s.put("sync", lambda e, o=b.stg_f[sl][:, :], i=m_src[e_, c * 128:(c + 1) * 128, :]:
                       e.dma_start(out=o, in_=i), waits=[("ve", stg_free[sl])],
                       inc=f"lw{sl}", amt=16)
            stg_free[sl] = s.put("vector", lambda e, o=m_view(b, e_, c), i=b.stg_f[sl][:, :]:
                                 e.tensor_copy(o, i), waits=[(f"lw{sl}", lr)], inc="ve")
    b.stgf_free[0] = stg_free[0]
    b.stgf_free[1] = stg_free[1]
    m_ready = stg_free[0] if stg_free[0] > stg_free[1] else stg_free[1]
    b.ln_state = {}
    xa_free = [0, 0]
    store_cnt_for_xn = [0, 0]
    pstr_free = 0
    psc_free = 0
    psum_half = [b.psA[:, 0:512], b.psA[:, 512:1024], b.psB[:, 0:512], b.psB[:, 512:1024]]
    bank6_free = [0, 0, 0, 0, 0, 0]
    moe_psc_read = 0
    lg_free = 0
    prev_trf = 0
    prev_trb = 0
    prev_xnv = 0
    prev_zdone = 0
    prev_ap1 = 0
    prev_logit = 0
    prev_expert = 0

    for g in range(32):
        sl = g % 2
        xr = s.put("sync", lambda e, o=b.xa[sl][:, :], i=xin[g * 128:(g + 1) * 128, :]:
                   e.dma_start(out=o, in_=i), waits=[("ve", xa_free[sl])],
                   inc=f"lx{sl}", amt=16)
        mean, rstd, sd = emit_ln(b, b.xa[sl], waits=[(f"lx{sl}", xr)])
        uv = s.put("vector", lambda e, o=b.u_f[:, :], i=b.xa[sl][:, :], m=mean, r=rstd:
                   e.tensor_scalar(o, i, m, r, ALU.subtract, ALU.mult),
                   waits=[("se", sd), ("pe", prev_trf), ("ve", prev_xnv)], inc="ve")
        b.ln_state["rstd_read"] = uv
        uh = s.put("vector", lambda e, o=b.u_h[:, :], i=b.u_f[:, :]: e.tensor_copy(o, i),
                   waits=[("pe", prev_trb), ("ve", uv)], inc="ve")
        for c in range(8):
            w = [("ve", uh)] + ([("ve", pstr_free), ("lc", b.ident_ld)] if c == 0 else [])
            kw = dict(inc="pe") if c == 7 else {}
            tr = s.put("tensor", lambda e, o=b.pstr[:, c * 128:(c + 1) * 128],
                       i=b.u_h[:, c * 128:(c + 1) * 128], d=b.identb[:, :]:
                       e.transpose(o, i, d), waits=w, **kw)
        u2h = s.put("vector", lambda e, o=b.u2T_h[:, :], i=b.pstr[:, :]: e.tensor_copy(o, i),
                    waits=[("pe", tr), ("pe", prev_expert)], inc="ve")
        pstr_free = u2h
        prev_trb = tr
        u2f = None
        for half in range(2):
            for c in range(4):
                cc = half * 4 + c
                w = [("ve", uv)] + ([("ve", psc_free), ("ve", moe_psc_read),
                                     ("lc", b.ident_ld)] if cc == 0 else [])
                kw = dict(inc="pe") if c == 3 else {}
                trf = s.put("tensor", lambda e, o=b.psC[:, half * 512 + c * 128: half * 512 + (c + 1) * 128],
                            i=b.u_f[:, cc * 128:(cc + 1) * 128], d=b.identf[:, :]:
                            e.transpose(o, i, d), waits=w, **kw)
            u2f = s.put("vector", lambda e, o=b.u2T_f[:, half * 512:(half + 1) * 512],
                        i=b.psC[:, half * 512:(half + 1) * 512]: e.tensor_copy(o, i),
                        waits=[("pe", trf), ("pe", prev_logit)], inc="ve")
        psc_free = u2f
        prev_trf = trf
        for d in range(8):
            w = [("ve", u2f), ("lc", gl)] if d == 0 else []
            if d == 0:
                w.append(("ve", lg_free))
            kw = dict(inc="pe") if d == 7 else {}
            lgd = s.put("tensor", lambda e, o=b.ps_lg[:, 0:8], a=b.u2T_f[:, d * 128:(d + 1) * 128],
                        r=b.g_sb[:, d * 8:(d + 1) * 8], st=(d == 0), sp=(d == 7):
                        e.matmul(o, a, r, start=st, stop=sp), waits=w, **kw)
        prev_logit = lgd
        lgc = s.put("vector", lambda e, o=sm(b, SM_LG, 8), i=b.ps_lg[:, 0:8]: e.tensor_copy(o, i),
                    waits=[("pe", lgd), ("ve", prev_zdone),
                           ("so", getattr(b, "dbg_so", 0) if DBG_SPILL_EXT else 0)], inc="ve")
        lg_free = lgc
        sc1 = s.put("scalar", lambda e, o=sm(b, SM_LGS, 8), i=sm(b, SM_LG, 8): e.copy(o, i),
                    waits=[("ve", lgc)], inc="se")
        mx = s.put("vector", lambda e, o=sm(b, SM_M1), i=sm(b, SM_LGS, 8): e.reduce_max(o, i, axis=AX),
                   waits=[("se", sc1)], inc="ve")
        sc2 = s.put("scalar", lambda e, o=sm(b, SM_M1S), i=sm(b, SM_M1): e.copy(o, i),
                    waits=[("ve", mx)], inc="se")
        eq1 = s.put("vector", lambda e, o=sm(b, SM_EQ1, 8), i=sm(b, SM_LGS, 8), m=sm(b, SM_M1S):
                    e.tensor_scalar(o, i, m, None, ALU.is_equal), waits=[("se", sc2)], inc="ve")
        sc3 = s.put("scalar", lambda e, o=sm(b, SM_EQ1S, 8), i=sm(b, SM_EQ1, 8): e.copy(o, i),
                    waits=[("ve", eq1)], inc="se")
        mk = s.put("vector", lambda e, o=sm(b, SM_MSK, 8), i=sm(b, SM_EQ1S, 8), j=sm(b, SM_LGS, 8):
                   e.scalar_tensor_tensor(o, i, -1e30, j, ALU.mult, ALU.add),
                   waits=[("se", sc3)], inc="ve")
        sc4 = s.put("scalar", lambda e, o=sm(b, SM_MSKS, 8), i=sm(b, SM_MSK, 8): e.copy(o, i),
                    waits=[("ve", mk)], inc="se")
        m2 = s.put("vector", lambda e, o=sm(b, SM_M2), i=sm(b, SM_MSKS, 8): e.reduce_max(o, i, axis=AX),
                   waits=[("se", sc4)], inc="ve")
        sc5 = s.put("scalar", lambda e, o=sm(b, SM_M2S), i=sm(b, SM_M2): e.copy(o, i),
                    waits=[("ve", m2)], inc="se")
        eq2 = s.put("vector", lambda e, o=sm(b, SM_EQ2, 8), i=sm(b, SM_MSKS, 8), m=sm(b, SM_M2S):
                    e.tensor_scalar(o, i, m, None, ALU.is_equal), waits=[("se", sc5)], inc="ve")
        sc6 = s.put("scalar", lambda e, o=sm(b, SM_EQ2S, 8), i=sm(b, SM_EQ2, 8): e.copy(o, i),
                    waits=[("ve", eq2)], inc="se")
        dm = s.put("vector", lambda e, o=sm(b, SM_DM), i=sm(b, SM_M1S), j=sm(b, SM_M2S):
                   e.tensor_sub(o, i, j), waits=[("se", sc6)], inc="ve")
        sg = s.put("scalar", lambda e, o=sm(b, SM_S1), i=sm(b, SM_DM):
                   e.activation(out=o, in_=i, func=AF.Sigmoid),
                   waits=[("ve", dm)], inc="se")
        s2v = s.put("vector", lambda e, o=sm(b, SM_S2), i=sm(b, SM_S1):
                    e.tensor_scalar(o, i, -1.0, 1.0, ALU.mult, ALU.add),
                    waits=[("se", sg)], inc="ve")
        sc7 = s.put("scalar", lambda e, o=sm(b, SM_S2S), i=sm(b, SM_S2): e.copy(o, i),
                    waits=[("ve", s2v)], inc="se")
        t2 = s.put("vector", lambda e, o=sm(b, SM_T2, 8), i=sm(b, SM_EQ2S, 8), m=sm(b, SM_S2S):
                   e.tensor_scalar(o, i, m, None, ALU.mult), waits=[("se", sc7)], inc="ve")
        sc8 = s.put("scalar", lambda e, o=sm(b, SM_T2S, 8), i=sm(b, SM_T2, 8): e.copy(o, i),
                    waits=[("ve", t2)], inc="se")
        gat = s.put("vector", lambda e, o=sm(b, SM_GAT, 8), i=sm(b, SM_EQ1S, 8), m=sm(b, SM_S1),
                    j=sm(b, SM_T2S, 8):
                    e.scalar_tensor_tensor(o, i, m, j, ALU.mult, ALU.add),
                    waits=[("se", sc8)], inc="ve")
        sc9 = s.put("scalar", lambda e, o=sm(b, SM_GATS, 8), i=sm(b, SM_GAT, 8): e.copy(o, i),
                    waits=[("ve", gat)], inc="se")
        if DBG_SPILL_EXT and "dbg_gat" in names:
            b.dbg_so = s.put("gpsimd", lambda e, o=names["dbg_gat"][g * 128:(g + 1) * 128, :],
                             i=sm(b, SM_M1, 48): e.dma_start(out=o, in_=i),
                             waits=[("se", sc9)], inc="so", amt=16)
        psum6 = [b.psA[:, 0:512], b.psA[:, 512:1024], b.psB[:, 0:512],
                 b.psB[:, 512:1024], b.psC[:, 0:512], b.psC[:, 512:1024]]
        last_evac = None
        for gidx in range(4):
            dh, grp = gidx // 2, gidx % 2
            hsel = [(4 * gidx + i) % 6 for i in range(4)]
            grp_done = [None] * 4
            for d in range(8):
                for j in range(4):
                    w = []
                    if d == 0:
                        w = [("ve", u2h), ("ve", m_ready), ("ve", bank6_free[hsel[j]])]
                        if hsel[j] >= 4:
                            w.append(("ve", psc_free))
                    kw = dict(inc="pe") if d == 7 else {}
                    grp_done[j] = s.put("tensor", lambda e, o=psum6[hsel[j]],
                                        a=b.u2T_h[:, d * 128:(d + 1) * 128],
                                        r=m_view(b, grp * 4 + j, d)[:, dh * 512:(dh + 1) * 512],
                                        st=(d == 0), sp=(d == 7):
                                        e.matmul(o, a, r, start=st, stop=sp), waits=w, **kw)
            for j in range(4):
                e_ = grp * 4 + j
                if e_ == 0:
                    last_evac = s.put("vector", lambda e, o=b.zb[:, dh * 512:(dh + 1) * 512],
                                      i=psum6[hsel[j]], m=sm(b, SM_GATS + e_):
                                      e.tensor_scalar(o, i, m, None, ALU.mult),
                                      waits=[("pe", grp_done[j]), ("se", sc9),
                                             ("ve", prev_ap1)], inc="ve")
                else:
                    last_evac = s.put("vector", lambda e, o=b.zb[:, dh * 512:(dh + 1) * 512],
                                      i=psum6[hsel[j]], m=sm(b, SM_GATS + e_),
                                      j2=b.zb[:, dh * 512:(dh + 1) * 512]:
                                      e.scalar_tensor_tensor(o, i, m, j2, ALU.mult, ALU.add),
                                      waits=[("pe", grp_done[j]), ("se", sc9)], inc="ve")
                bank6_free[hsel[j]] = last_evac
                if hsel[j] >= 4:
                    moe_psc_read = last_evac
        zdone = last_evac
        prev_zdone = zdone
        prev_expert = grp_done[3]
        mean, rstd, sd = emit_ln(b, b.zb, waits=[("ve", zdone)])
        ap1 = s.put("vector", lambda e, o=b.u_f[:, :], i=b.zb[:, :], m=mean, r=rstd:
                    e.tensor_scalar(o, i, m, r, ALU.subtract, ALU.mult),
                    waits=[("se", sd), ("ve", uh)], inc="ve")
        b.ln_state["rstd_read"] = ap1
        prev_ap1 = ap1
        xnv = s.put("vector", lambda e, o=b.xn[sl][:, :], i=b.u_f[:, :], j=b.xa[sl][:, :]:
                    e.tensor_add(o, i, j),
                    waits=[(f"sx{sl}", store_cnt_for_xn[sl]), ("ve", ap1)], inc="ve")
        prev_xnv = xnv
        xa_free[sl] = xnv
        store_cnt_for_xn[sl] = s.put("gpsimd", lambda e, o=xout[g * 128:(g + 1) * 128, :],
                                     i=b.xn[sl][:, :]: e.dma_start(out=o, in_=i),
                                     waits=[("ve", xnv)], inc=f"sx{sl}", amt=16)


def emit_spec(b, names, xin, xout):
    s = b.sch
    bl = s.put("sync", lambda e, o=b.bias_row, i=names["s_b1"][:, :]: e.dma_start(out=o, in_=i),
               inc="lc", amt=16)
    f0 = s.fence()
    emit_bias_bcast(b, b.bvb, b.bias_row, extra_waits=s.wf(f0))

    def cast_in(dst_off, src, neg=False, extra=()):
        done = 0
        for c in range(8):
            sl = c % 2
            w = list(extra) if c == 0 else []
            w.append(("ve", cast_in.free[sl]))
            lr = s.put("sync", lambda e, o=b.stg_f[sl][:, :], i=src[c * 128:(c + 1) * 128, :]:
                       e.dma_start(out=o, in_=i), waits=w, inc=f"lw{sl}", amt=16)
            if neg:
                done = s.put("vector", lambda e, o=b.big[:, dst_off + c * 1024: dst_off + (c + 1) * 1024],
                             i=b.stg_f[sl][:, :]: e.tensor_scalar(o, i, -1.0, None, ALU.mult),
                             waits=[(f"lw{sl}", lr)], inc="ve")
            else:
                done = s.put("vector", lambda e, o=b.big[:, dst_off + c * 1024: dst_off + (c + 1) * 1024],
                             i=b.stg_f[sl][:, :]: e.tensor_copy(o, i),
                             waits=[(f"lw{sl}", lr)], inc="ve")
            cast_in.free[sl] = done
        return done

    cast_in.free = [b.stgf_free[0], b.stgf_free[1]]
    cast_in(SP_WR, names["s_wr"])
    cast_in(SP_WI, names["s_wi"])
    cast_in(SP_WNEG, names["s_wi"], neg=True)
    cast_in(SP_FR, names["c_fr"])
    c_all = cast_in(SP_FI, names["c_fi"])

    def big_at(off, c, lo, n):
        return b.big[:, off + c * 1024 + lo: off + c * 1024 + lo + n]

    bank_free = [0, 0]
    half = [b.psA[:, 0:512], b.psA[:, 512:1024]]
    for tgt, terms in ((SP_AT, ((SP_WR, SP_FR), (SP_WNEG, SP_FI))),
                      (None, ((SP_WI, SP_FR), (SP_WR, SP_FI)))):
        for oc in range(8):
            for ih in range(2):
                j = (oc * 2 + ih) % 2
                mmd = None
                first = True
                for (wsrc, fsrc) in terms:
                    for d in range(8):
                        w = [("ve", c_all), ("ve", bank_free[j])] if first else []
                        last = (wsrc, d) == (terms[1][0], 7)
                        kw = dict(inc="pe") if last else {}
                        mmd = s.put("tensor", lambda e, o=half[j],
                                    a=big_at(wsrc, d, oc * 128, 128),
                                    r=big_at(fsrc, d, ih * 512, 512),
                                    st=first, sp=last:
                                    e.matmul(o, a, r, start=st, stop=sp), waits=w, **kw)
                        first = False
                if tgt is not None:
                    dst = b.big[:, tgt + oc * 1024 + ih * 512: tgt + oc * 1024 + ih * 512 + 512]
                else:
                    dst = b.wbuf[:, oc * 1024 + ih * 512: oc * 1024 + ih * 512 + 512]
                bank_free[j] = s.put("vector", lambda e, o=dst, i=half[j]: e.tensor_copy(o, i),
                                     waits=[("pe", mmd)], inc="ve")
    f1 = s.fence()
    cast_in.free = [f1["ve"], f1["ve"]]
    cast_in(SP_GR, names["c_gr"], extra=[("pe", f1["pe"])])
    g_all = cast_in(SP_GIN, names["c_gin"])
    b.stgf_free[0] = max(b.stgf_free[0], cast_in.free[0])
    b.stgf_free[1] = max(b.stgf_free[1], cast_in.free[1])
    for ic in range(8):
        for jh in range(2):
            j = (ic * 2 + jh) % 2
            mmd = None
            first = True
            for (msrc, gsrc) in ((SP_AT, SP_GR), (None, SP_GIN)):
                for o_ in range(8):
                    w = [("ve", g_all), ("ve", bank_free[j])] if first else []
                    last = (gsrc == SP_GIN and o_ == 7)
                    kw = dict(inc="pe") if last else {}
                    if msrc is not None:
                        a_ap = b.big[:, SP_AT + o_ * 1024 + ic * 128: SP_AT + o_ * 1024 + ic * 128 + 128]
                    else:
                        a_ap = b.wbuf[:, o_ * 1024 + ic * 128: o_ * 1024 + ic * 128 + 128]
                    mmd = s.put("tensor", lambda e, o=half[j], a=a_ap,
                                r=big_at(gsrc, o_, jh * 512, 512), st=first, sp=last:
                                e.matmul(o, a, r, start=st, stop=sp), waits=w, **kw)
                    first = False
            dst = b.big[:, SP_WEFF + ic * 1024 + jh * 512: SP_WEFF + ic * 1024 + jh * 512 + 512]
            bank_free[j] = s.put("vector", lambda e, o=dst, i=half[j]: e.tensor_copy(o, i),
                                 waits=[("pe", mmd)], inc="ve")
    weff_ready = bank_free[0] if bank_free[0] > bank_free[1] else bank_free[1]
    if DBG_SPILL_EXT and "dbg_weff" in names:
        s.put("gpsimd", lambda e, o=names["dbg_weff"][0:128, :],
              i=b.big[:, SP_WEFF:SP_WEFF + 8192]: e.dma_start(out=o, in_=i),
              waits=[("ve", weff_ready)], inc="so", amt=16)
        s.put("gpsimd", lambda e, o=names["dbg_weff"][128:256, :],
              i=b.big[:, SP_AT:SP_AT + 8192]: e.dma_start(out=o, in_=i), inc="so", amt=16)
        s.put("gpsimd", lambda e, o=names["dbg_weff"][256:384, :],
              i=b.wbuf[:, :]: e.dma_start(out=o, in_=i), inc="so", amt=16)

    b.ln_state = {}
    xa_free = [0, 0]
    store_cnt_for_xn = [0, 0]
    pstr_free = 0
    pbank_free = [0, 0]
    prev_trv = 0
    prev_mm = 0
    prev_xnv = 0
    prev_ap = 0
    for g in range(32):
        sl = g % 2
        xr = s.put("sync", lambda e, o=b.xa[sl][:, :], i=xin[g * 128:(g + 1) * 128, :]:
                   e.dma_start(out=o, in_=i), waits=[("ve", xa_free[sl])],
                   inc=f"lx{sl}", amt=16)
        mean, rstd, sd = emit_ln(b, b.xa[sl], waits=[(f"lx{sl}", xr)])
        uv = s.put("vector", lambda e, o=b.u_f[:, :], i=b.xa[sl][:, :], m=mean, r=rstd:
                   e.tensor_scalar(o, i, m, r, ALU.subtract, ALU.mult),
                   waits=[("se", sd), ("ve", prev_xnv)], inc="ve")
        b.ln_state["rstd_read"] = uv
        uh = s.put("vector", lambda e, o=b.u_h[:, :], i=b.u_f[:, :]: e.tensor_copy(o, i),
                   waits=[("pe", prev_trv), ("ve", uv)], inc="ve")
        for c in range(8):
            w = [("ve", uh)] + ([("ve", pstr_free), ("lc", b.ident_ld)] if c == 0 else [])
            kw = dict(inc="pe") if c == 7 else {}
            tr = s.put("tensor", lambda e, o=b.pstr[:, c * 128:(c + 1) * 128],
                       i=b.u_h[:, c * 128:(c + 1) * 128], d=b.identb[:, :]:
                       e.transpose(o, i, d), waits=w, **kw)
        u2h = s.put("vector", lambda e, o=b.u2T_h[:, :], i=b.pstr[:, :]: e.tensor_copy(o, i),
                    waits=[("pe", tr), ("pe", prev_mm)], inc="ve")
        pstr_free = u2h
        prev_trv = tr
        pod = [None, None]
        for dh in range(2):
            for d in range(8):
                w = []
                if d == 0:
                    w = [("ve", u2h), ("ve", weff_ready), ("ve", pbank_free[dh])]
                kw = dict(inc="pe") if d == 7 else {}
                pod[dh] = s.put("tensor", lambda e, o=half[dh],
                                a=b.u2T_h[:, d * 128:(d + 1) * 128],
                                r=b.big[:, SP_WEFF + d * 1024 + dh * 512: SP_WEFF + d * 1024 + dh * 512 + 512],
                                st=(d == 0), sp=(d == 7):
                                e.matmul(o, a, r, start=st, stop=sp), waits=w, **kw)
        ev0 = s.put("vector", lambda e, o=b.zb[:, 0:512], i=half[0], j=b.bvb[:, 0:512]:
                    e.tensor_tensor(o, i, j, ALU.add),
                    waits=[("pe", pod[0]), ("ve", prev_ap)], inc="ve")
        ev1 = s.put("vector", lambda e, o=b.zb[:, 512:1024], i=half[1], j=b.bvb[:, 512:1024]:
                    e.tensor_tensor(o, i, j, ALU.add), waits=[("pe", pod[1])], inc="ve")
        pbank_free[0] = ev0
        pbank_free[1] = ev1
        prev_mm = pod[1]
        mean, rstd, sd = emit_ln(b, b.zb, waits=[("ve", ev1)])
        ap1 = s.put("vector", lambda e, o=b.tmpa[:, :], i=b.zb[:, 0:512], m=mean, r=rstd:
                    e.tensor_scalar(o, i, m, r, ALU.subtract, ALU.mult),
                    waits=[("se", sd), ("ve", prev_xnv)], inc="ve")
        ap2 = s.put("vector", lambda e, o=b.tmpb[:, :], i=b.zb[:, 512:1024], m=mean, r=rstd:
                    e.tensor_scalar(o, i, m, r, ALU.subtract, ALU.mult),
                    waits=[("ve", ap1)], inc="ve")
        b.ln_state["rstd_read"] = ap2
        prev_ap = ap2
        x1 = s.put("vector", lambda e, o=b.xn[sl][:, 0:512], i=b.tmpa[:, :], j=b.xa[sl][:, 0:512]:
                   e.tensor_add(o, i, j),
                   waits=[(f"sx{sl}", store_cnt_for_xn[sl]), ("ve", ap2)], inc="ve")
        xnv = s.put("vector", lambda e, o=b.xn[sl][:, 512:1024], i=b.tmpb[:, :], j=b.xa[sl][:, 512:1024]:
                    e.tensor_add(o, i, j), waits=[("ve", x1)], inc="ve")
        prev_xnv = xnv
        xa_free[sl] = xnv
        store_cnt_for_xn[sl] = s.put("gpsimd", lambda e, o=xout[g * 128:(g + 1) * 128, :],
                                     i=b.xn[sl][:, :]: e.dma_start(out=o, in_=i),
                                     waits=[("ve", xnv)], inc=f"sx{sl}", amt=16)


def build_launch(segs):
    nc = bass.Bass("TRN2", target_bir_lowering=False)
    names = {}

    ext_ins = []

    def din(name, shape, dt=F32):
        names[name] = nc.dram_tensor(name, shape, dt, kind="ExternalInput").ap()
        ext_ins.append(name)

    def dout(name, shape, dt=F32):
        names[name] = nc.dram_tensor(name, shape, dt, kind="ExternalOutput").ap()

    def dint(name, shape, dt):
        names[name] = nc.dram_tensor(name, shape, dt, kind="Internal").ap()

    din("c_identb", [128, 128], BF16)
    needs_identf = any(s == "moe" for s in segs)
    if needs_identf:
        din("c_identf", [128, 128], F32)
    has_stream = any(s0 in ("attn0", "attn1", "moe", "spec") for s0 in segs)
    if has_stream:
        din("x_in", [S, D])
        dout("y", [S, D])
    for sg in segs:
        if sg in ("attn0", "attn1"):
            L = 0 if sg == "attn0" else 1
            for w in ("wqT", "wkT", "wvT", "woT"):
                din(f"a{L}_{w}", [D, D])
            din(f"a{L}_bqc", [128, 8])
            din(f"a{L}_bkc", [128, 8])
            din(f"a{L}_bv", [1, D])
            din(f"a{L}_bo", [1, D])
            if L == 0:
                din("rcos", [S, 512])
                din("rsin", [S, 512])
            if "qspill" not in names:
                if DBG_SPILL_EXT:
                    dout("qspill", [D, S], BF16)
                    dout("kspill", [D, S], BF16)
                    dout("dbg_rxq", [512, 8192], BF16)
                    dout("dbg_uh", [S, D], BF16)
                    dout("dbg_uf", [S, D], F32)
                    dout("dbg_ag", [S, 4], F32)
                    dout("dbg_att", [128, 4096], BF16)
                    dout("dbg_vaug", [128, 33280], BF16)
                    dout("dbg_wk", [128, 8192], BF16)
                    dout("dbg_wv", [128, 8192], BF16)
                    dout("dbg_wq", [128, 8192], BF16)
                else:
                    dint("qspill", [D, S], BF16)
                    dint("kspill", [D, S], BF16)
        elif sg == "mbuild":
            din("mb_win", [4096, 1024])
            din("mb_woutT", [4096, 1024])
            dout("m_out", [1024, 1024])
        elif sg == "mbuild8":
            din("m_win8", [8, 4096, 1024])
            din("m_woutT8", [8, 4096, 1024])
            dint("m_spill", [8, 1024, 1024], F32)
        elif sg == "moe":
            din("g_wTc", [128, 64])
            if "mbuild8" not in segs:
                din("m_all", [8, 1024, 1024])
            if DBG_SPILL_EXT:
                dout("dbg_gat", [S, 48])
        elif sg == "spec":
            if DBG_SPILL_EXT:
                dout("dbg_weff", [384, 8192], BF16)
            din("s_wr", [D, D])
            din("s_wi", [D, D])
            din("s_b1", [1, D])
            din("c_fr", [D, D])
            din("c_fi", [D, D])
            din("c_gr", [D, D])
            din("c_gin", [D, D])

    sch = Sch(DMA_SEMS + ["pe", "ve", "se"])
    with ExitStack() as ctx:
        b = alloc_tensors(nc, ctx)
        b.sch = sch
        b.ln_state = {}
        sems = {}
        for s0 in DMA_SEMS + ["pe", "ve", "se"]:
            sems[s0] = ctx.enter_context(nc.semaphore(s0))
        sch.sems = sems
        emit_consts(b, names)
        xi, xo = (names.get("x_in"), names.get("y"))
        stream_segs = [s0 for s0 in segs if s0 not in ("mbuild", "mbuild8")]
        cur_in = xi

        def seg_fence():
            f = sch.fence()
            for e0 in ENGINES:
                sch.put(e0, lambda e: None, waits=sch.wf(f))
            b.wr_fence = f["pe"]

        for i, sg in enumerate(segs):
            if i > 0:
                seg_fence()
            if sg == "mbuild":
                emit_mbuild(b, names,
                            lambda hk, wout=False, a=names["mb_win"], bb=names["mb_woutT"]:
                                (bb if wout else a)[hk * 128:(hk + 1) * 128, :],
                            None,
                            lambda ch, dh, t=names["m_out"]: t[ch * 128:(ch + 1) * 128, dh * 512:(dh + 1) * 512])
                continue
            if sg == "mbuild8":
                for e_ in range(8):
                    emit_mbuild(b, names,
                                lambda hk, wout=False, E=e_, a=names["m_win8"], bb=names["m_woutT8"]:
                                    (bb if wout else a)[E, hk * 128:(hk + 1) * 128, :],
                                None,
                                lambda ch, dh, E=e_, t=names["m_spill"]: t[E, ch * 128:(ch + 1) * 128, dh * 512:(dh + 1) * 512])
                continue
            last = (sg == stream_segs[-1])
            if last:
                cur_out = xo
            else:
                nm_mid = f"x_mid{i}"
                names[nm_mid] = nc.dram_tensor(nm_mid, [S, D], F32, kind="Internal").ap()
                cur_out = names[nm_mid]
            if sg == "attn0":
                emit_attn(b, names, 0, cur_in, cur_out)
            elif sg == "attn1":
                emit_attn(b, names, 1, cur_in, cur_out)
            elif sg == "moe":
                emit_moe(b, names, cur_in, cur_out,
                         m_src=names.get("m_spill", names.get("m_all")))
            elif sg == "spec":
                emit_spec(b, names, cur_in, cur_out)
            cur_in = cur_out
        fin = sch.fence()
        sch.put("gpsimd", lambda e: None,
                waits=[(s0, fin[s0]) for s0 in ("sk0", "sk1", "sx0", "sx1", "so")])
        block = ctx.enter_context(nc.Block())
        block.sync(lambda e: sch.run("sync", e))
        block.tensor(lambda e: sch.run("tensor", e))
        block.vector(lambda e: sch.run("vector", e))
        block.scalar(lambda e: sch.run("scalar", e))
        block.gpsimd(lambda e: sch.run("gpsimd", e))
    return nc, names, ext_ins



def _host_prep(inputs):
    import ml_dtypes
    c = {}
    f32 = np.float32
    asc = np.ascontiguousarray
    for L, p in ((0, "L0_attn_"), (1, "L1_attn_")):
        c[f"a{L}_wqT"] = asc(np.asarray(inputs[p + "Wq"], f32).T)
        c[f"a{L}_wkT"] = asc(np.asarray(inputs[p + "Wk"], f32).T)
        c[f"a{L}_wvT"] = asc(np.asarray(inputs[p + "Wv"], f32).T)
        c[f"a{L}_woT"] = asc(np.asarray(inputs[p + "Wo"], f32).T)
        c[f"a{L}_bqc"] = asc(np.asarray(inputs[p + "bq"], f32).reshape(8, 128).T)
        c[f"a{L}_bkc"] = asc(np.asarray(inputs[p + "bk"], f32).reshape(8, 128).T)
        c[f"a{L}_bv"] = asc(np.asarray(inputs[p + "bv"], f32).reshape(1, D))
        c[f"a{L}_bo"] = asc(np.asarray(inputs[p + "bo"], f32).reshape(1, D))
    freq = np.exp(np.arange(0, D, 2, dtype=np.float64) * (-np.log(10000.0) / D))
    ang = np.arange(S, dtype=np.float64)[:, None] * freq[None, :]
    c["rcos"] = np.cos(ang).astype(f32)
    c["rsin"] = np.sin(ang).astype(f32)
    c["g_wTc"] = asc(np.asarray(inputs["L0_ffn_w_gate"], f32).T.reshape(8, 128, 8)
                     .transpose(1, 0, 2).reshape(128, 64))
    c["m_win"] = np.asarray(inputs["L0_ffn_w_in"], f32)
    c["m_woutT"] = asc(np.asarray(inputs["L0_ffn_w_out"], f32).transpose(0, 2, 1))
    c["s_wr"] = np.asarray(inputs["L1_ffn_w_freq_real"], f32)
    c["s_wi"] = np.asarray(inputs["L1_ffn_w_freq_imag"], f32)
    c["s_b1"] = asc(np.asarray(inputs["L1_ffn_bias"], f32).reshape(1, D))
    dd = np.arange(D, dtype=np.float64)
    th = 2.0 * np.pi * np.outer(dd, dd) / D
    c["c_fr"] = np.cos(th).astype(f32)
    c["c_fi"] = (-np.sin(th)).astype(f32)
    c["c_gr"] = (np.cos(th) / D).astype(f32)
    c["c_gin"] = (-np.sin(th) / D).astype(f32)
    c["c_identb"] = np.eye(128, dtype=ml_dtypes.bfloat16)
    c["c_identf"] = np.eye(128, dtype=f32)
    c["x"] = np.asarray(inputs["L0_attn_x"], f32)
    return c


def _in_maps_for(ext_ins, c, core, xcur=None, m_all=None):
    m = {}
    for k in ext_ins:
        if k == "x_in":
            m[k] = xcur[core]
        elif k == "mb_win":
            m[k] = c["m_win"][core]
        elif k == "mb_woutT":
            m[k] = c["m_woutT"][core]
        elif k == "m_win8":
            m[k] = c["m_win"]
        elif k == "m_woutT8":
            m[k] = c["m_woutT"]
        elif k == "m_all":
            m[k] = m_all
        else:
            m[k] = c[k]
    return m


FUSED = True


def kernel(**inputs):
    c = _host_prep(inputs)
    xcur = c["x"]

    def launch(segs, xcur=None, m_all=None, outkey="y"):
        nc, names, ext_ins = build_launch(segs)
        maps = [_in_maps_for(ext_ins, c, core, xcur=xcur, m_all=m_all) for core in range(NCORES)]
        res = run_bass_kernel_spmd(nc, maps, core_ids=list(range(NCORES)))
        return [np.asarray(r[outkey]) for r in res.results]

    if FUSED:
        y = launch(["attn0", "mbuild8", "moe", "attn1", "spec"], xcur=xcur)
        return np.stack(y, axis=0)

    x1 = launch(["attn0"], xcur=xcur)
    m_outs = launch(["mbuild"], xcur=None, outkey="m_out")
    m_all = np.stack(m_outs, axis=0)
    x1 = np.stack(x1, axis=0)
    x2 = np.stack(launch(["moe"], xcur=x1, m_all=m_all), axis=0)
    x3 = np.stack(launch(["attn1"], xcur=x2), axis=0)
    y = np.stack(launch(["spec"], xcur=x3), axis=0)
    return y


# version 14 (final)
# speedup vs baseline: 1.0603x; 1.0603x over previous
import numpy as np
from contextlib import ExitStack
import concourse.bass as bass
import concourse.mybir as mybir
from concourse.bass_utils import run_bass_kernel_spmd

F32 = mybir.dt.float32
BF16 = mybir.dt.bfloat16
AX = mybir.AxisListType.X
ALU = mybir.AluOpType
AF = mybir.ActivationFunctionType

S, D, H, HD = 4096, 1024, 16, 64
DC = D // 128
NT = S // 128
EPS = 1e-5
NCORES = 8

ENGINES = ("sync", "tensor", "vector", "scalar", "gpsimd")
DBG_SPILL_EXT = False
DMA_SEMS = ["lc", "lx0", "lx1", "lr0", "lw0", "lw1", "lg0", "lg1",
            "lk0", "lk1", "lq0", "lq1", "sk0", "sk1", "sx0", "sx1", "so", "gp"]


class Sch:

    def __init__(self, sem_names):
        self.items = {e: [] for e in ENGINES}
        self.cnt = {s: 0 for s in sem_names}
        self.sems = {}

    def put(self, eng, fn, waits=(), inc=None, amt=1):
        tgt = None
        if inc is not None:
            self.cnt[inc] += amt
            tgt = self.cnt[inc]
        ws = tuple((s, v) for (s, v) in waits if v is not None and v > 0)
        self.items[eng].append((ws, fn, inc, amt))
        return tgt

    def fence(self):
        return dict(self.cnt)

    def wf(self, f):
        return [(s, v) for s, v in f.items() if v > 0]

    def run(self, eng_name, eng):
        seen = {}
        for waits, fn, inc, amt in self.items[eng_name]:
            for s, v in waits:
                if seen.get(s, 0) >= v:
                    continue
                eng.wait_ge(self.sems[s], v)
                seen[s] = v
            ins = fn(eng)
            if inc is not None and ins is not None:
                ins.then_inc(self.sems[inc], amt)


class B:
    pass


SM_BN, SM_AG, SM_LNV, SM_RSTD = 0, 12, 14, 15
SM_M1, SM_M2, SM_DM, SM_S1, SM_S2 = 16, 17, 18, 19, 20
SM_LG, SM_EQ1, SM_MSK, SM_EQ2, SM_GAT = 24, 32, 40, 48, 56
SM_BQC, SM_BKC = 64, 72
SM_LGS, SM_M1S, SM_M2S, SM_S2S = 80, 88, 89, 90
SM_T2, SM_EQ1S, SM_EQ2S, SM_MSKS, SM_T2S, SM_GATS = 96, 104, 112, 120, 128, 136

BB_COLS = 57344
A_VAUG = 0
A_KBUF = 33280
A_QBUF0 = 37376
A_QBUF1 = 41472
A_RXQ = 45568
A_ATT = 45568
A_PT0 = 49664
A_PT1 = 51712
SP_WR, SP_WI, SP_WNEG, SP_FR, SP_FI = 0, 8192, 16384, 24576, 32768
SP_AT = 40960
SP_GR, SP_GIN = 0, 8192
SP_WEFF = 49152


def alloc_tensors(nc, ctx):
    b = B()
    b.nc = nc
    sb = lambda n, s, d: ctx.enter_context(nc.sbuf_tensor(n, s, d))
    psa = lambda n, s, d: ctx.enter_context(nc.psum_tensor(n, s, d))
    b.big = sb("big_bf", [128, BB_COLS], BF16)
    b.wbuf = sb("wbuf", [128, 8192], BF16)
    b.u_h = sb("u_h", [128, 1024], BF16)
    b.u2T_h = sb("u2T_h", [128, 1024], BF16)
    b.kstg = [sb("kstg0", [128, 512], BF16), sb("kstg1", [128, 512], BF16)]
    b.identb = sb("identb", [128, 128], BF16)
    b.mb_a = [sb("mb_a0", [128, 1024], BF16), sb("mb_a1", [128, 1024], BF16)]
    b.mb_b = [sb("mb_b0", [128, 512], BF16), sb("mb_b1", [128, 512], BF16)]
    b.xa = [sb("xa0", [128, 1024], F32), sb("xa1", [128, 1024], F32)]
    b.u_f = sb("u_f", [128, 1024], F32)
    b.zb = sb("zb", [128, 1024], F32)
    b.xn = [sb("xn0", [128, 1024], F32), sb("xn1", [128, 1024], F32)]
    b.tmpa = sb("tmpa", [128, 512], F32)
    b.tmpb = sb("tmpb", [128, 512], F32)
    b.rc = sb("rc", [128, 512], F32)
    b.rs = sb("rs", [128, 512], F32)
    b.bvb = sb("bvb", [128, 1024], F32)
    b.u2T_f = sb("u2T_f", [128, 1024], F32)
    b.stg_f = [sb("stg_f0", [128, 1024], F32), sb("stg_f1", [128, 1024], F32)]
    b.stg_g = [sb("stg_g0", [128, 512], F32), sb("stg_g1", [128, 512], F32)]
    b.rows = sb("rows", [128, 2048], F32)
    b.small = sb("small", [128, 160], F32)
    b.g_sb = sb("g_sb", [128, 64], F32)
    b.bc_sb = sb("bc_sb", [128, 512], F32)
    b.att_sc = sb("att_sc", [128, 1024], F32)
    b.tmpc = b.att_sc[:, 0:512]
    b.tmpd = b.att_sc[:, 512:1024]
    b.identf = sb("identf", [128, 128], F32)
    b.epsc = sb("epsc", [128, 1], F32)
    b.psA = psa("psA", [128, 1024], F32)
    b.psB = psa("psB", [128, 1024], F32)
    b.psC = psa("psC", [128, 1024], F32)
    b.ps_lg = psa("ps_lg", [128, 512], F32)
    b.pstr = psa("pstr", [128, 1024], BF16)
    b.onesf = b.rows[0:1, 0:512]
    b.stgf_free = [0, 0]
    b.stgg_free = [0, 0]
    b.ident_ld = 0
    b.rcp = b.rows[0:1, 512:1024]
    b.bias_row = b.rows[0:1, 1024:2048]
    return b


def sm(b, off, n=1):
    return b.small[:, off:off + n]


def emit_consts(b, names):
    s = b.sch
    b.ident_ld = s.put("sync", lambda e, o=b.identb[:, :], i=names["c_identb"][:, :]:
                       e.dma_start(out=o, in_=i), inc="lc", amt=16)
    if "c_identf" in names:
        b.ident_ld = s.put("sync", lambda e, o=b.identf[:, :], i=names["c_identf"][:, :]:
                           e.dma_start(out=o, in_=i), inc="lc", amt=16)
    s.put("vector", lambda e, o=b.onesf: e.memset(o, 1.0), inc="ve")
    s.put("vector", lambda e, o=b.epsc[:, :]: e.memset(o, EPS), inc="ve")
    b.nc.const_aps.register(F32, EPS, b.epsc[:, 0:1])


def emit_ln(b, src_ap, waits):
    s = b.sch
    st = b.ln_state
    s.put("vector", lambda e, o=sm(b, SM_BN, 6), i=src_ap[:, 0:512]: e.bn_stats(o, i),
          waits=list(waits) + [("ve", st.get("aggr_done", 0))])
    v1 = s.put("vector", lambda e, o=sm(b, SM_BN + 6, 6), i=src_ap[:, 512:1024]:
               e.bn_stats(o, i), inc="ve")
    v2 = s.put("vector", lambda e, o=sm(b, SM_AG, 2), i=sm(b, SM_BN, 12):
               e.bn_aggr(o, i),
               waits=[("se", st.get("ln_read", 0)), ("ve", v1),
                      ("ve", st.get("rstd_read", 0))], inc="ve")
    st["aggr_done"] = v2
    s1 = s.put("scalar", lambda e, o=sm(b, SM_LNV), i=sm(b, SM_AG + 1):
               e.activation(out=o, in_=i, func=AF.Ln, bias=EPS),
               waits=[("ve", v2)], inc="se")
    st["ln_read"] = s1
    s2 = s.put("scalar", lambda e, o=sm(b, SM_RSTD), i=sm(b, SM_LNV):
               e.activation(out=o, in_=i, func=AF.Exp, scale=-0.5),
               waits=[("se", s1), ("ve", st.get("rstd_read", 0))], inc="se")
    return sm(b, SM_AG), sm(b, SM_RSTD), s2


def emit_bias_bcast(b, dst, row_ap, extra_waits=()):
    s = b.sch
    w = list(extra_waits)
    s.put("tensor", lambda e, o=b.psC[:, 0:512], a=b.onesf[0:1, 0:128], r=row_ap[0:1, 0:512]:
          e.matmul(o, a, r, start=True, stop=True), waits=w)
    p = s.put("tensor", lambda e, o=b.psC[:, 512:1024], a=b.onesf[0:1, 0:128], r=row_ap[0:1, 512:1024]:
              e.matmul(o, a, r, start=True, stop=True), inc="pe")
    v = s.put("vector", lambda e, o=dst[:, :], i=b.psC[:, :]: e.tensor_copy(o, i),
              waits=[("pe", p)], inc="ve")
    return v


def emit_attn(b, names, L, xin, xout):
    s = b.sch
    nm = lambda k: names[f"a{L}_{k}"]
    qspill, kspill = names["qspill"], names["kspill"]
    rope = (L == 0)

    s.put("sync", lambda e, o=sm(b, SM_BQC, 8), i=nm("bqc")[:, :]: e.dma_start(out=o, in_=i), inc="lc", amt=16)
    s.put("sync", lambda e, o=sm(b, SM_BKC, 8), i=nm("bkc")[:, :]: e.dma_start(out=o, in_=i), inc="lc", amt=16)
    s.put("sync", lambda e, o=b.bias_row, i=nm("bv")[:, :]: e.dma_start(out=o, in_=i), inc="lc", amt=16)
    f0 = s.fence()
    emit_bias_bcast(b, b.bvb, b.bias_row, extra_waits=s.wf(f0))
    vmemset = s.put("vector", lambda e, o=b.big[:, A_VAUG:A_VAUG + 33280]: e.memset(o, 1.0),
                    waits=s.wf(f0), inc="ve")

    xa_free = [0, 0]
    rope_free = 0
    pstr_free = 0
    prev_trv = 0
    bank_free = [0, 0, 0, 0]
    kstg_free = [0, 0]
    b.ln_state = {}

    for Q in range(4):
        rxq_done = None
        for t in range(8):
            g = Q * 8 + t
            sl = g % 2
            xr = s.put("sync", lambda e, o=b.xa[sl][:, :], i=xin[g * 128:(g + 1) * 128, :]:
                       e.dma_start(out=o, in_=i), waits=[("ve", xa_free[sl])],
                       inc=f"lx{sl}", amt=16)
            if rope:
                s.put("sync", lambda e, o=b.rc[:, :], i=names["rcos"][g * 128:(g + 1) * 128, :]:
                      e.dma_start(out=o, in_=i), waits=[("ve", rope_free)], inc="lr0", amt=16)
                rr = s.put("sync", lambda e, o=b.rs[:, :], i=names["rsin"][g * 128:(g + 1) * 128, :]:
                           e.dma_start(out=o, in_=i), inc="lr0", amt=16)
            mean, rstd, sd = emit_ln(b, b.xa[sl], waits=[(f"lx{sl}", xr)])
            if DBG_SPILL_EXT and "dbg_ag" in names and L == 0:
                s.put("gpsimd", lambda e, o=names["dbg_ag"][g * 128:(g + 1) * 128, :],
                      i=sm(b, SM_AG, 4): e.dma_start(out=o, in_=i),
                      waits=[("se", sd)], inc="so", amt=16)
            uv = s.put("vector", lambda e, o=b.u_f[:, :], i=b.xa[sl][:, :], m=mean, r=rstd:
                       e.tensor_scalar(o, i, m, r, ALU.subtract, ALU.mult),
                       waits=[("se", sd), ("ve", rope_free)], inc="ve")
            b.ln_state["rstd_read"] = uv
            xa_free[sl] = uv
            if DBG_SPILL_EXT and "dbg_uf" in names and L == 0:
                s.put("gpsimd", lambda e, o=names["dbg_uf"][g * 128:(g + 1) * 128, :],
                      i=b.u_f[:, :]: e.dma_start(out=o, in_=i),
                      waits=[("ve", uv)], inc="so", amt=16)
            if rope:
                g1 = s.put("gpsimd", lambda e, o=b.tmpa[:, :], i=b.u_f[:, 0:512], j=b.rc[:, :]:
                           e.tensor_mul(o, i, j),
                           waits=[("lr0", rr), ("pe", prev_trv), ("ve", rope_free), ("ve", uv)])
                g2 = s.put("gpsimd", lambda e, o=b.tmpb[:, :], i=b.u_f[:, 512:1024], j=b.rs[:, :]:
                           e.tensor_mul(o, i, j), inc="gp")
                g3 = s.put("gpsimd", lambda e, o=b.tmpc[:, :], i=b.u_f[:, 0:512], j=b.rs[:, :]:
                           e.tensor_mul(o, i, j))
                g4 = s.put("gpsimd", lambda e, o=b.tmpd[:, :], i=b.u_f[:, 512:1024], j=b.rc[:, :]:
                           e.tensor_mul(o, i, j), inc="gp")
                r3 = s.put("vector", lambda e, o=b.u_h[:, 0:512], i=b.tmpa[:, :], j=b.tmpb[:, :]:
                           e.tensor_sub(o, i, j), waits=[("gp", g2)], inc="ve")
                uh = s.put("vector", lambda e, o=b.u_h[:, 512:1024], i=b.tmpc[:, :], j=b.tmpd[:, :]:
                           e.tensor_add(o, i, j), waits=[("gp", g4)], inc="ve")
                rope_free = uh
            else:
                uh = s.put("vector", lambda e, o=b.u_h[:, :], i=b.u_f[:, :]: e.tensor_copy(o, i),
                           waits=[("pe", prev_trv), ("ve", uv)], inc="ve")
            for c in range(8):
                w = [("ve", uh)] + ([("ve", pstr_free), ("lc", b.ident_ld)] if c == 0 else [])
                kw = dict(inc="pe") if c == 7 else {}
                tr = s.put("tensor", lambda e, o=b.pstr[:, c * 128:(c + 1) * 128],
                           i=b.u_h[:, c * 128:(c + 1) * 128], d=b.identb[:, :]:
                           e.transpose(o, i, d), waits=w, **kw)
            if DBG_SPILL_EXT and "dbg_uh" in names and L == 0:
                s.put("gpsimd", lambda e, o=names["dbg_uh"][g * 128:(g + 1) * 128, :],
                      i=b.u_h[:, :]: e.dma_start(out=o, in_=i),
                      waits=[("ve", uh)], inc="so", amt=16)
            s.put("tensor", lambda e, o=b.ps_lg[:, :], a=b.identb[:, :],
                  r=b.u_h[:, 0:512]: e.matmul(o, a, r, start=True, stop=True))
            dst = b.big[:, A_RXQ + t * 128: A_RXQ + t * 128 + 8 * 1024]
            dst = dst.rearrange("p (c x) -> p c x", x=1024)[:, :, 0:128]
            rxq_done = s.put("vector", lambda e, o=dst, i=b.pstr[:, :].rearrange("p (c x) -> p c x", x=128):
                             e.tensor_copy(o, i), waits=[("pe", tr)], inc="ve")
            pstr_free = rxq_done
            prev_trv = tr

        if DBG_SPILL_EXT and "dbg_rxq" in names and L == 0:
            s.put("gpsimd", lambda e, o=names["dbg_rxq"][Q * 128:(Q + 1) * 128, :],
                  i=b.big[:, A_RXQ:A_RXQ + 8192]: e.dma_start(out=o, in_=i),
                  waits=[("ve", rxq_done)], inc="so", amt=16)
        wbuf_free = b.wr_fence if hasattr(b, "wr_fence") else 0
        psum_half = [b.psA[:, 0:512], b.psA[:, 512:1024], b.psB[:, 0:512], b.psB[:, 512:1024]]

        for wsel in ("k", "v", "q"):
            wt = nm("w" + wsel + "T")
            for c in range(8):
                lr = s.put("sync", lambda e, o=b.stg_f[c % 2][:, :], i=wt[c * 128:(c + 1) * 128, :]:
                           e.dma_start(out=o, in_=i), waits=[("ve", b.stgf_free[c % 2])],
                           inc=f"lw{c % 2}", amt=16)
                b.stgf_free[c % 2] = s.put("vector", lambda e, o=b.wbuf[:, c * 1024:(c + 1) * 1024],
                                           i=b.stg_f[c % 2][:, :]: e.tensor_copy(o, i),
                                           waits=[(f"lw{c % 2}", lr), ("pe", wbuf_free)], inc="ve")
            cast_all = b.stgf_free[1]
            if DBG_SPILL_EXT and f"dbg_w{wsel}" in names and L == 0 and Q == 0:
                s.put("gpsimd", lambda e, o=names[f"dbg_w{wsel}"][:, :], i=b.wbuf[:, :]:
                      e.dma_start(out=o, in_=i), waits=[("ve", cast_all)], inc="so", amt=16)
            if wsel in ("k", "q"):
                spill = kspill if wsel == "k" else qspill
                bcol = SM_BQC if wsel == "q" else SM_BKC
                for tokN in range(2):
                    for dg in range(2):
                        grp_done = [None] * 4
                        for d in range(8):
                            for j in range(4):
                                w = []
                                if d == 0:
                                    w = [("ve", cast_all), ("ve", rxq_done), ("ve", bank_free[j])]
                                kw = dict(inc="pe") if d == 7 else {}
                                grp_done[j] = s.put("tensor", lambda e, o=psum_half[j],
                                                    a=b.wbuf[:, d * 1024 + (dg * 4 + j) * 128: d * 1024 + (dg * 4 + j) * 128 + 128],
                                                    r=b.big[:, A_RXQ + d * 1024 + tokN * 512: A_RXQ + d * 1024 + tokN * 512 + 512],
                                                    st=(d == 0), sp=(d == 7):
                                                    e.matmul(o, a, r, start=st, stop=sp), waits=w, **kw)
                        for j in range(4):
                            ch = dg * 4 + j
                            if wsel == "q":
                                ev = s.put("vector", lambda e, o=b.kstg[j % 2][:, :], i=psum_half[j],
                                           m=sm(b, bcol + ch):
                                           e.tensor_scalar(o, i, m, 0.125, ALU.add, ALU.mult),
                                           waits=[("pe", grp_done[j]), (f"sk{j % 2}", kstg_free[j % 2])], inc="ve")
                            else:
                                ev = s.put("vector", lambda e, o=b.kstg[j % 2][:, :], i=psum_half[j],
                                           m=sm(b, bcol + ch):
                                           e.tensor_scalar(o, i, m, None, ALU.add),
                                           waits=[("pe", grp_done[j]), (f"sk{j % 2}", kstg_free[j % 2])], inc="ve")
                            bank_free[j] = ev
                            kstg_free[j % 2] = s.put("gpsimd", lambda e, o=spill[ch * 128:(ch + 1) * 128, Q * 1024 + tokN * 512: Q * 1024 + tokN * 512 + 512],
                                                     i=b.kstg[j % 2][:, :]: e.dma_start(out=o, in_=i),
                                                     waits=[("ve", ev)], inc=f"sk{j % 2}", amt=16)
            else:
                for t in range(8):
                    g = Q * 8 + t
                    for dh in range(2):
                        j = (t * 2 + dh) % 4
                        grp = None
                        for d in range(8):
                            w = []
                            if d == 0:
                                w = [("ve", cast_all), ("ve", rxq_done), ("ve", bank_free[j]),
                                     ("ve", vmemset)]
                            kw = dict(inc="pe") if d == 7 else {}
                            grp = s.put("tensor", lambda e, o=psum_half[j],
                                        a=b.big[:, A_RXQ + d * 1024 + t * 128: A_RXQ + d * 1024 + t * 128 + 128],
                                        r=b.wbuf[:, d * 1024 + dh * 512: d * 1024 + dh * 512 + 512],
                                        st=(d == 0), sp=(d == 7):
                                        e.matmul(o, a, r, start=st, stop=sp), waits=w, **kw)
                        voff = A_VAUG + g * 1040 + dh * 8 * 65
                        dst = b.big[:, voff:voff + 520].rearrange("p (h x) -> p h x", x=65)[:, :, 0:64]
                        bank_free[j] = s.put("vector", lambda e, o=dst,
                                             i=psum_half[j].rearrange("p (h x) -> p h x", x=64),
                                             bb=b.bvb[:, dh * 512:dh * 512 + 512].rearrange("p (h x) -> p h x", x=64):
                                             e.tensor_tensor(o, i, bb, ALU.add),
                                             waits=[("pe", grp)], inc="ve")
            fw = s.fence()
            wbuf_free = fw["pe"]
        b.wr_fence = wbuf_free

    fA = s.fence()
    if DBG_SPILL_EXT and "dbg_vaug" in names and L == 0:
        s.put("gpsimd", lambda e, o=names["dbg_vaug"][:, :], i=b.big[:, A_VAUG:A_VAUG + 33280]:
              e.dma_start(out=o, in_=i), waits=s.wf(fA), inc="so", amt=16)
    s.put("sync", lambda e, o=b.bias_row, i=nm("bo")[:, :]: e.dma_start(out=o, in_=i),
          waits=[("pe", fA["pe"])], inc="lc", amt=16)
    fB0 = s.fence()
    bobv = emit_bias_bcast(b, b.bvb, b.bias_row, extra_waits=s.wf(fB0))
    for c in range(8):
        lw = [("ve", b.stgf_free[c % 2])]
        lr = s.put("sync", lambda e, o=b.stg_f[c % 2][:, :], i=nm("woT")[c * 128:(c + 1) * 128, :]:
                   e.dma_start(out=o, in_=i), waits=lw, inc=f"lw{c % 2}", amt=16)
        b.stgf_free[c % 2] = s.put("vector", lambda e, o=b.wbuf[:, c * 1024:(c + 1) * 1024],
                                   i=b.stg_f[c % 2][:, :]: e.tensor_copy(o, i),
                                   waits=[(f"lw{c % 2}", lr)] + ([("pe", fA["pe"])] if c == 0 else []),
                                   inc="ve")
    wo_ready = b.stgf_free[1]

    qdone = [fA["pe"], fA["pe"]]
    kb_free = [fA["pe"], fA["pe"]]
    qrdy = [0, 0]
    ps_se = [fA["se"], fA["se"]]
    ps_ve = [bobv, bobv]
    acc_ve = [bobv, bobv]
    pT_free = [fA["pe"], fA["pe"]]
    tail_free = [fA["ve"], fA["ve"]]
    tt_prev = [fA["ve"]]
    bcc_free = [bobv]
    rcp_se_free = fA["se"]
    rcp_free = [fA["pe"]]
    pend_tail = None
    attT_pe_free = fA["pe"]
    acc_free = bobv
    xa_free = [max(xa_free[0], 0), max(xa_free[1], 0)]
    store_cnt_for_xn = [0, 0]

    for qb in range(8):
        qsl = qb % 2
        qoff = (A_QBUF0, A_QBUF1)[qsl]
        for c in range(8):
            w = ([("pe", qdone[qsl]), ("sk0", fA["sk0"]), ("sk1", fA["sk1"])]
                 if c == 0 else [])
            qrdy[qsl] = s.put("sync", lambda e, o=b.big[:, qoff + c * 512: qoff + (c + 1) * 512],
                              i=qspill[c * 128:(c + 1) * 128, qb * 512:(qb + 1) * 512]:
                              e.dma_start(out=o, in_=i), waits=w, inc=f"lq{qsl}", amt=16)
        for h in range(16):
            hp = h % 2
            krdy = s.put("sync", lambda e,
                         o=b.big[hp * 64:hp * 64 + 64, A_KBUF:A_KBUF + 4096],
                         i=kspill[h * 64:(h + 1) * 64, :]:
                         e.dma_start(out=o, in_=i),
                         waits=[("pe", kb_free[hp]), ("sk0", fA["sk0"]), ("sk1", fA["sk1"])],
                         inc=f"lk{hp}", amt=16)
            kb = b.big[hp * 64:hp * 64 + 64, A_KBUF:A_KBUF + 4096]
            qch = b.big[hp * 64:hp * 64 + 64, qoff + (h // 2) * 512: qoff + (h // 2) * 512 + 512]
            pts = [b.big[:, A_PT0:A_PT0 + 1024], b.big[:, A_PT1:A_PT1 + 1024]]
            psX = [b.psA[:, :], b.psB[:, :]]
            acc = b.psC[:, 0:512]
            sd = {}
            av = {}
            ex = {}

            def sfill(u, first=False):
                sl2 = u % 2
                for c2 in range(2):
                    kt = 2 * u + c2
                    w0 = []
                    if c2 == 0:
                        w0 = [("se", ex.get(u - 2, ps_se[sl2])), ("ve", ps_ve[sl2])]
                        if first:
                            w0 += [(f"lk{hp}", krdy), (f"lq{qsl}", qrdy[qsl])]
                    kw = dict(inc="pe") if c2 == 1 else {}
                    last = s.put("tensor", lambda e, o=psX[sl2][:, c2 * 512:(c2 + 1) * 512],
                                 a=kb[:, kt * 128:kt * 128 + 128], r=qch:
                                 e.matmul(o, a, r, start=True, stop=True), waits=w0, **kw)
                    if c2 == 1:
                        sd[u] = last

            def avrun(u):
                sl2 = u % 2
                for c2 in range(2):
                    kt = 2 * u + c2
                    w0 = [("se", ex[u])] + ([("ve", acc_ve[0])] if u == 0 else []) if c2 == 0 else []
                    kw = dict(inc="pe") if c2 == 1 else {}
                    r_ = s.put("tensor", lambda e, o=acc[0:65, :],
                               a=b.big[:, A_VAUG + kt * 1040 + h * 65: A_VAUG + kt * 1040 + h * 65 + 65],
                               r=pts[sl2][:, c2 * 512:(c2 + 1) * 512],
                               st=(u == 0 and c2 == 0), sp=(u == 15 and c2 == 1):
                               e.matmul(o, a, r, start=st, stop=sp), waits=w0, **kw)
                    if c2 == 1:
                        av[u] = r_

            sfill(0, first=True)
            sfill(1, first=True)
            if pend_tail is not None:
                pend_tail()
                pend_tail = None
            for u in range(16):
                ex[u] = s.put("scalar", lambda e, o=pts[u % 2], i=psX[u % 2][:, :]:
                              e.activation(out=o, in_=i, func=AF.Exp),
                              waits=[("pe", sd[u]), ("pe", av.get(u - 2, pT_free[u % 2]))],
                              inc="se")
                avrun(u)
                if u + 2 < 16:
                    sfill(u + 2)
                s.put("tensor", lambda e, o=b.ps_lg[:, :], a=b.identb[:, :],
                      r=b.big[:, A_VAUG:A_VAUG + 512]:
                      e.matmul(o, a, r, start=True, stop=True))
            kb_free[hp] = sd[15]
            ps_se[0] = ex[14]
            ps_se[1] = ex[15]
            pT_free[0] = av[14]
            pT_free[1] = av[15]
            slT = hp
            dA = s.put("vector", lambda e, o=b.att_sc[0:65, slT * 512:slT * 512 + 512],
                       i=acc[0:65, :]:
                       e.tensor_copy(o, i), waits=[("pe", av[15]), ("ve", tail_free[slT])],
                       inc="ve")
            acc_ve[0] = dA
            sLn = s.put("scalar", lambda e, o=b.bc_sb[64:65, :],
                        i=b.att_sc[64:65, slT * 512:slT * 512 + 512]:
                        e.activation(out=o, in_=i, func=AF.Ln),
                        waits=[("ve", dA), ("se", rcp_se_free)], inc="se")
            sRc = s.put("scalar", lambda e, o=b.rcp, i=b.bc_sb[64:65, :]:
                        e.activation(out=o, in_=i, func=AF.Exp, scale=-1.0),
                        waits=[("se", sLn), ("pe", rcp_free[0])], inc="se")
            rcp_se_free = sRc

            def make_tail(h=h, hp=hp, slT=slT, sRc=sRc):
                def part2():
                    nonlocal_vars = None
                    bcd = s.put("tensor", lambda e, o=b.psC[0:64, 512:1024],
                                a=b.onesf[0:1, 0:64], r=b.rcp:
                                e.matmul(o, a, r, start=True, stop=True),
                                waits=[("se", sRc), ("ve", bcc_free[0])], inc="pe")
                    rcp_free[0] = bcd
                    cC = s.put("vector", lambda e, o=b.bc_sb[0:64, :], i=b.psC[0:64, 512:1024]:
                               e.tensor_copy(o, i), waits=[("pe", bcd), ("ve", tt_prev[0])],
                               inc="ve")
                    bcc_free[0] = cC
                    attv = b.big[hp * 64:hp * 64 + 64,
                                 A_ATT + (h // 2) * 512: A_ATT + (h // 2) * 512 + 512]
                    tt = s.put("vector", lambda e, o=attv,
                               i=b.att_sc[0:64, slT * 512:slT * 512 + 512], j=b.bc_sb[0:64, :]:
                               e.tensor_tensor(o, i, j, ALU.mult),
                               waits=[("ve", cC), ("pe", attT_pe_free)], inc="ve")
                    tt_prev[0] = tt
                    tail_free[slT] = tt
                    return tt
                return part2

            pend_tail = make_tail()
        if pend_tail is not None:
            attT_ready = pend_tail()
            pend_tail = None
        qdone[qsl] = sd[15]

        prev_ev = 0
        prev_ap1 = 0
        for t in range(4):
            g = qb * 4 + t
            sl = t % 2
            xr = s.put("sync", lambda e, o=b.xa[sl][:, :], i=xin[g * 128:(g + 1) * 128, :]:
                       e.dma_start(out=o, in_=i), waits=[("ve", xa_free[sl])],
                       inc=f"lx{sl}", amt=16)
            po = [b.psC[:, 0:512], b.psC[:, 512:1024]]
            pod = [None, None]
            for dh in range(2):
                for d in range(8):
                    w = []
                    if dh == 0 and d == 0:
                        w = [("ve", attT_ready), ("se", ps_se[0]), ("se", ps_se[1]),
                             ("ve", wo_ready), ("ve", prev_ev)]
                    kw = dict(inc="pe") if d == 7 else {}
                    pod[dh] = s.put("tensor", lambda e, o=po[dh],
                                    a=b.big[:, A_ATT + d * 512 + t * 128: A_ATT + d * 512 + t * 128 + 128],
                                    r=b.wbuf[:, d * 1024 + dh * 512: d * 1024 + dh * 512 + 512],
                                    st=(d == 0), sp=(d == 7):
                                    e.matmul(o, a, r, start=st, stop=sp), waits=w, **kw)
            ev0 = s.put("vector", lambda e, o=b.zb[:, 0:512], i=po[0], j=b.bvb[:, 0:512]:
                        e.tensor_tensor(o, i, j, ALU.add),
                        waits=[("pe", pod[0]), ("ve", bobv), ("ve", prev_ap1)], inc="ve")
            ev1 = s.put("vector", lambda e, o=b.zb[:, 512:1024], i=po[1], j=b.bvb[:, 512:1024]:
                        e.tensor_tensor(o, i, j, ALU.add), waits=[("pe", pod[1])], inc="ve")
            prev_ev = ev1
            mean, rstd, sd = emit_ln(b, b.zb, waits=[("ve", ev1)])
            ap1 = s.put("vector", lambda e, o=b.u_f[:, :], i=b.zb[:, :], m=mean, r=rstd:
                        e.tensor_scalar(o, i, m, r, ALU.subtract, ALU.mult),
                        waits=[("se", sd), (f"lx{sl}", xr)], inc="ve")
            b.ln_state["rstd_read"] = ap1
            prev_ap1 = ap1
            xnv = s.put("vector", lambda e, o=b.xn[sl][:, :], i=b.u_f[:, :], j=b.xa[sl][:, :]:
                        e.tensor_add(o, i, j),
                        waits=[(f"sx{sl}", store_cnt_for_xn[sl]), ("ve", ap1)], inc="ve")
            xa_free[sl] = xnv
            store_cnt_for_xn[sl] = s.put("gpsimd", lambda e, o=xout[g * 128:(g + 1) * 128, :],
                                         i=b.xn[sl][:, :]: e.dma_start(out=o, in_=i),
                                         waits=[("ve", xnv)], inc=f"sx{sl}", amt=16)
        attT_pe_free = s.cnt["pe"]
        acc_ve[0] = s.cnt["ve"]
        acc_ve[1] = s.cnt["ve"]
    b.wr_fence = s.cnt["pe"]


def mb_wout_view(b, hk):
    if hk < 24:
        return b.big[:, 32768 + hk * 1024: 32768 + (hk + 1) * 1024]
    return b.wbuf[:, (hk - 24) * 1024:(hk - 23) * 1024]


def emit_mbuild(b, names, win, woutT, mout):
    s = b.sch
    psum_half = [b.psA[:, 0:512], b.psA[:, 512:1024], b.psB[:, 0:512], b.psB[:, 512:1024],
                 b.psC[:, 0:512], b.psC[:, 512:1024], b.ps_lg[:, :],
                 b.pstr[:, :].bitcast(F32)]
    if not hasattr(b, "mb_state"):
        b.mb_state = dict(ev=[0, 0], st=[0, 0], mmdone=0)
    ms = b.mb_state
    ev_free = ms["ev"]
    st_free = ms["st"]
    cast_done = {}
    stgg_free = b.stgg_free
    for hk in range(32):
        sl = hk % 2
        lr = s.put("sync", lambda e, o=b.stg_f[sl][:, :], i=win(hk, False):
                   e.dma_start(out=o, in_=i), waits=[("ve", b.stgf_free[sl])],
                   inc=f"lw{sl}", amt=16)
        b.stgf_free[sl] = s.put("vector", lambda e, o=b.big[:, hk * 1024:(hk + 1) * 1024],
                                i=b.stg_f[sl][:, :]: e.tensor_copy(o, i),
                                waits=[(f"lw{sl}", lr), ("pe", ms["mmdone"])], inc="ve")
        lg0 = s.put("gpsimd", lambda e, o=b.stg_g[0][:, :], i=win(hk, True)[:, 0:512]:
                    e.dma_start(out=o, in_=i), waits=[("ve", stgg_free[0])],
                    inc="lg0", amt=16)
        lg1 = s.put("gpsimd", lambda e, o=b.stg_g[1][:, :], i=win(hk, True)[:, 512:1024]:
                    e.dma_start(out=o, in_=i), waits=[("ve", stgg_free[1])],
                    inc="lg1", amt=16)
        stgg_free[0] = s.put("vector", lambda e, o=mb_wout_view(b, hk)[:, 0:512],
                             i=b.stg_g[0][:, :]: e.tensor_copy(o, i),
                             waits=[("lg0", lg0), ("pe", ms["mmdone"])], inc="ve")
        stgg_free[1] = s.put("vector", lambda e, o=mb_wout_view(b, hk)[:, 512:1024],
                             i=b.stg_g[1][:, :]: e.tensor_copy(o, i),
                             waits=[("lg1", lg1), ("pe", ms["mmdone"])], inc="ve")
        cast_done[hk] = stgg_free[1]
    cast_last = cast_done[31]
    first_pass = True
    for dh in range(2):
        grp_done = [None] * 8
        for hk in range(32):
            for j in range(8):
                w = []
                if j == 0 and first_pass:
                    w = [("ve", cast_done[hk])]
                if hk == 0 and j == 0:
                    w += [("ve", ev_free[0]), ("ve", ev_free[1])]
                    if not first_pass:
                        w += [("ve", cast_last)]
                kw = dict(inc="pe") if (hk == 31 and j == 7) else {}
                grp_done[j] = s.put("tensor", lambda e, o=psum_half[j],
                                    a=b.big[:, hk * 1024 + j * 128: hk * 1024 + j * 128 + 128],
                                    r=mb_wout_view(b, hk)[:, dh * 512:(dh + 1) * 512],
                                    st=(hk == 0), sp=(hk == 31):
                                    e.matmul(o, a, r, start=st, stop=sp), waits=w, **kw)
        first_pass = False
        pe_end = s.cnt["pe"]
        ms["mmdone"] = pe_end
        for j in range(8):
            ch = j
            sl = j % 2
            ev = s.put("vector", lambda e, o=b.stg_g[sl][:, :], i=psum_half[j]:
                       e.tensor_copy(o, i), waits=[("pe", pe_end), (f"sk{sl}", st_free[sl])],
                       inc="ve")
            stgg_free[sl] = ev
            ev_free[sl] = ev
            st_free[sl] = s.put("gpsimd", lambda e,
                                o=mout(ch, dh),
                                i=b.stg_g[sl][:, :]: e.dma_start(out=o, in_=i),
                                waits=[("ve", ev)], inc=f"sk{sl}", amt=16)


def m_view(b, e_, dc):
    idx = e_ * 8 + dc
    if idx < 56:
        return b.big[:, idx * 1024:(idx + 1) * 1024]
    return b.wbuf[:, (idx - 56) * 1024:(idx - 55) * 1024]


def emit_moe(b, names, xin, xout, m_src=None):
    s = b.sch
    if m_src is None:
        m_src = names["m_all"]
    gl = s.put("sync", lambda e, o=b.g_sb[:, :], i=names["g_wTc"][:, :]:
               e.dma_start(out=o, in_=i), inc="lc", amt=16)
    stg_free = [b.stgf_free[0], b.stgf_free[1]]
    for e_ in range(8):
        for c in range(8):
            sl = (e_ * 8 + c) % 2
            lr = s.put("sync", lambda e, o=b.stg_f[sl][:, :], i=m_src[e_, c * 128:(c + 1) * 128, :]:
                       e.dma_start(out=o, in_=i), waits=[("ve", stg_free[sl])],
                       inc=f"lw{sl}", amt=16)
            stg_free[sl] = s.put("vector", lambda e, o=m_view(b, e_, c), i=b.stg_f[sl][:, :]:
                                 e.tensor_copy(o, i), waits=[(f"lw{sl}", lr)], inc="ve")
    b.stgf_free[0] = stg_free[0]
    b.stgf_free[1] = stg_free[1]
    m_ready = stg_free[0] if stg_free[0] > stg_free[1] else stg_free[1]
    b.ln_state = {}
    xa_free = [0, 0]
    store_cnt_for_xn = [0, 0]
    pstr_free = 0
    psc_free = 0
    psum_half = [b.psA[:, 0:512], b.psA[:, 512:1024], b.psB[:, 0:512], b.psB[:, 512:1024]]
    bank6_free = [0, 0, 0, 0, 0, 0]
    moe_psc_read = 0
    lg_free = 0
    prev_trf = 0
    prev_trb = 0
    prev_xnv = 0
    prev_zdone = 0
    prev_ap1 = 0
    prev_logit = 0
    prev_expert = 0

    for g in range(32):
        sl = g % 2
        xr = s.put("sync", lambda e, o=b.xa[sl][:, :], i=xin[g * 128:(g + 1) * 128, :]:
                   e.dma_start(out=o, in_=i), waits=[("ve", xa_free[sl])],
                   inc=f"lx{sl}", amt=16)
        mean, rstd, sd = emit_ln(b, b.xa[sl], waits=[(f"lx{sl}", xr)])
        uv = s.put("vector", lambda e, o=b.u_f[:, :], i=b.xa[sl][:, :], m=mean, r=rstd:
                   e.tensor_scalar(o, i, m, r, ALU.subtract, ALU.mult),
                   waits=[("se", sd), ("pe", prev_trf), ("ve", prev_xnv)], inc="ve")
        b.ln_state["rstd_read"] = uv
        uh = s.put("vector", lambda e, o=b.u_h[:, :], i=b.u_f[:, :]: e.tensor_copy(o, i),
                   waits=[("pe", prev_trb), ("ve", uv)], inc="ve")
        for c in range(8):
            w = [("ve", uh)] + ([("ve", pstr_free), ("lc", b.ident_ld)] if c == 0 else [])
            kw = dict(inc="pe") if c == 7 else {}
            tr = s.put("tensor", lambda e, o=b.pstr[:, c * 128:(c + 1) * 128],
                       i=b.u_h[:, c * 128:(c + 1) * 128], d=b.identb[:, :]:
                       e.transpose(o, i, d), waits=w, **kw)
        u2h = s.put("vector", lambda e, o=b.u2T_h[:, :], i=b.pstr[:, :]: e.tensor_copy(o, i),
                    waits=[("pe", tr), ("pe", prev_expert)], inc="ve")
        pstr_free = u2h
        prev_trb = tr
        u2f = None
        for half in range(2):
            for c in range(4):
                cc = half * 4 + c
                w = [("ve", uv)] + ([("ve", psc_free), ("ve", moe_psc_read),
                                     ("lc", b.ident_ld)] if cc == 0 else [])
                kw = dict(inc="pe") if c == 3 else {}
                trf = s.put("tensor", lambda e, o=b.psC[:, half * 512 + c * 128: half * 512 + (c + 1) * 128],
                            i=b.u_f[:, cc * 128:(cc + 1) * 128], d=b.identf[:, :]:
                            e.transpose(o, i, d), waits=w, **kw)
            u2f = s.put("vector", lambda e, o=b.u2T_f[:, half * 512:(half + 1) * 512],
                        i=b.psC[:, half * 512:(half + 1) * 512]: e.tensor_copy(o, i),
                        waits=[("pe", trf), ("pe", prev_logit)], inc="ve")
        psc_free = u2f
        prev_trf = trf
        for d in range(8):
            w = [("ve", u2f), ("lc", gl)] if d == 0 else []
            if d == 0:
                w.append(("ve", lg_free))
            kw = dict(inc="pe") if d == 7 else {}
            lgd = s.put("tensor", lambda e, o=b.ps_lg[:, 0:8], a=b.u2T_f[:, d * 128:(d + 1) * 128],
                        r=b.g_sb[:, d * 8:(d + 1) * 8], st=(d == 0), sp=(d == 7):
                        e.matmul(o, a, r, start=st, stop=sp), waits=w, **kw)
        prev_logit = lgd
        lgc = s.put("vector", lambda e, o=sm(b, SM_LG, 8), i=b.ps_lg[:, 0:8]: e.tensor_copy(o, i),
                    waits=[("pe", lgd), ("ve", prev_zdone),
                           ("so", getattr(b, "dbg_so", 0) if DBG_SPILL_EXT else 0)], inc="ve")
        lg_free = lgc
        sc1 = s.put("scalar", lambda e, o=sm(b, SM_LGS, 8), i=sm(b, SM_LG, 8): e.copy(o, i),
                    waits=[("ve", lgc)], inc="se")
        mx = s.put("vector", lambda e, o=sm(b, SM_M1), i=sm(b, SM_LGS, 8): e.reduce_max(o, i, axis=AX),
                   waits=[("se", sc1)], inc="ve")
        sc2 = s.put("scalar", lambda e, o=sm(b, SM_M1S), i=sm(b, SM_M1): e.copy(o, i),
                    waits=[("ve", mx)], inc="se")
        eq1 = s.put("vector", lambda e, o=sm(b, SM_EQ1, 8), i=sm(b, SM_LGS, 8), m=sm(b, SM_M1S):
                    e.tensor_scalar(o, i, m, None, ALU.is_equal), waits=[("se", sc2)], inc="ve")
        sc3 = s.put("scalar", lambda e, o=sm(b, SM_EQ1S, 8), i=sm(b, SM_EQ1, 8): e.copy(o, i),
                    waits=[("ve", eq1)], inc="se")
        mk = s.put("vector", lambda e, o=sm(b, SM_MSK, 8), i=sm(b, SM_EQ1S, 8), j=sm(b, SM_LGS, 8):
                   e.scalar_tensor_tensor(o, i, -1e30, j, ALU.mult, ALU.add),
                   waits=[("se", sc3)], inc="ve")
        sc4 = s.put("scalar", lambda e, o=sm(b, SM_MSKS, 8), i=sm(b, SM_MSK, 8): e.copy(o, i),
                    waits=[("ve", mk)], inc="se")
        m2 = s.put("vector", lambda e, o=sm(b, SM_M2), i=sm(b, SM_MSKS, 8): e.reduce_max(o, i, axis=AX),
                   waits=[("se", sc4)], inc="ve")
        sc5 = s.put("scalar", lambda e, o=sm(b, SM_M2S), i=sm(b, SM_M2): e.copy(o, i),
                    waits=[("ve", m2)], inc="se")
        eq2 = s.put("vector", lambda e, o=sm(b, SM_EQ2, 8), i=sm(b, SM_MSKS, 8), m=sm(b, SM_M2S):
                    e.tensor_scalar(o, i, m, None, ALU.is_equal), waits=[("se", sc5)], inc="ve")
        sc6 = s.put("scalar", lambda e, o=sm(b, SM_EQ2S, 8), i=sm(b, SM_EQ2, 8): e.copy(o, i),
                    waits=[("ve", eq2)], inc="se")
        dm = s.put("vector", lambda e, o=sm(b, SM_DM), i=sm(b, SM_M1S), j=sm(b, SM_M2S):
                   e.tensor_sub(o, i, j), waits=[("se", sc6)], inc="ve")
        sg = s.put("scalar", lambda e, o=sm(b, SM_S1), i=sm(b, SM_DM):
                   e.activation(out=o, in_=i, func=AF.Sigmoid),
                   waits=[("ve", dm)], inc="se")
        s2v = s.put("vector", lambda e, o=sm(b, SM_S2), i=sm(b, SM_S1):
                    e.tensor_scalar(o, i, -1.0, 1.0, ALU.mult, ALU.add),
                    waits=[("se", sg)], inc="ve")
        sc7 = s.put("scalar", lambda e, o=sm(b, SM_S2S), i=sm(b, SM_S2): e.copy(o, i),
                    waits=[("ve", s2v)], inc="se")
        t2 = s.put("vector", lambda e, o=sm(b, SM_T2, 8), i=sm(b, SM_EQ2S, 8), m=sm(b, SM_S2S):
                   e.tensor_scalar(o, i, m, None, ALU.mult), waits=[("se", sc7)], inc="ve")
        sc8 = s.put("scalar", lambda e, o=sm(b, SM_T2S, 8), i=sm(b, SM_T2, 8): e.copy(o, i),
                    waits=[("ve", t2)], inc="se")
        gat = s.put("vector", lambda e, o=sm(b, SM_GAT, 8), i=sm(b, SM_EQ1S, 8), m=sm(b, SM_S1),
                    j=sm(b, SM_T2S, 8):
                    e.scalar_tensor_tensor(o, i, m, j, ALU.mult, ALU.add),
                    waits=[("se", sc8)], inc="ve")
        sc9 = s.put("scalar", lambda e, o=sm(b, SM_GATS, 8), i=sm(b, SM_GAT, 8): e.copy(o, i),
                    waits=[("ve", gat)], inc="se")
        if DBG_SPILL_EXT and "dbg_gat" in names:
            b.dbg_so = s.put("gpsimd", lambda e, o=names["dbg_gat"][g * 128:(g + 1) * 128, :],
                             i=sm(b, SM_M1, 48): e.dma_start(out=o, in_=i),
                             waits=[("se", sc9)], inc="so", amt=16)
        psum6 = [b.psA[:, 0:512], b.psA[:, 512:1024], b.psB[:, 0:512],
                 b.psB[:, 512:1024], b.psC[:, 0:512], b.psC[:, 512:1024]]
        last_evac = None
        for gidx in range(4):
            dh, grp = gidx // 2, gidx % 2
            hsel = [(4 * gidx + i) % 6 for i in range(4)]
            grp_done = [None] * 4
            for d in range(8):
                for j in range(4):
                    w = []
                    if d == 0:
                        w = [("ve", u2h), ("ve", m_ready), ("ve", bank6_free[hsel[j]])]
                        if hsel[j] >= 4:
                            w.append(("ve", psc_free))
                    kw = dict(inc="pe") if d == 7 else {}
                    grp_done[j] = s.put("tensor", lambda e, o=psum6[hsel[j]],
                                        a=b.u2T_h[:, d * 128:(d + 1) * 128],
                                        r=m_view(b, grp * 4 + j, d)[:, dh * 512:(dh + 1) * 512],
                                        st=(d == 0), sp=(d == 7):
                                        e.matmul(o, a, r, start=st, stop=sp), waits=w, **kw)
            for j in range(4):
                e_ = grp * 4 + j
                if e_ == 0:
                    last_evac = s.put("vector", lambda e, o=b.zb[:, dh * 512:(dh + 1) * 512],
                                      i=psum6[hsel[j]], m=sm(b, SM_GATS + e_):
                                      e.tensor_scalar(o, i, m, None, ALU.mult),
                                      waits=[("pe", grp_done[j]), ("se", sc9),
                                             ("ve", prev_ap1)], inc="ve")
                else:
                    last_evac = s.put("vector", lambda e, o=b.zb[:, dh * 512:(dh + 1) * 512],
                                      i=psum6[hsel[j]], m=sm(b, SM_GATS + e_),
                                      j2=b.zb[:, dh * 512:(dh + 1) * 512]:
                                      e.scalar_tensor_tensor(o, i, m, j2, ALU.mult, ALU.add),
                                      waits=[("pe", grp_done[j]), ("se", sc9)], inc="ve")
                bank6_free[hsel[j]] = last_evac
                if hsel[j] >= 4:
                    moe_psc_read = last_evac
        zdone = last_evac
        prev_zdone = zdone
        prev_expert = grp_done[3]
        mean, rstd, sd = emit_ln(b, b.zb, waits=[("ve", zdone)])
        ap1 = s.put("vector", lambda e, o=b.u_f[:, :], i=b.zb[:, :], m=mean, r=rstd:
                    e.tensor_scalar(o, i, m, r, ALU.subtract, ALU.mult),
                    waits=[("se", sd), ("ve", uh)], inc="ve")
        b.ln_state["rstd_read"] = ap1
        prev_ap1 = ap1
        xnv = s.put("vector", lambda e, o=b.xn[sl][:, :], i=b.u_f[:, :], j=b.xa[sl][:, :]:
                    e.tensor_add(o, i, j),
                    waits=[(f"sx{sl}", store_cnt_for_xn[sl]), ("ve", ap1)], inc="ve")
        prev_xnv = xnv
        xa_free[sl] = xnv
        store_cnt_for_xn[sl] = s.put("gpsimd", lambda e, o=xout[g * 128:(g + 1) * 128, :],
                                     i=b.xn[sl][:, :]: e.dma_start(out=o, in_=i),
                                     waits=[("ve", xnv)], inc=f"sx{sl}", amt=16)


def emit_spec(b, names, xin, xout):
    s = b.sch
    bl = s.put("sync", lambda e, o=b.bias_row, i=names["s_b1"][:, :]: e.dma_start(out=o, in_=i),
               inc="lc", amt=16)
    f0 = s.fence()
    emit_bias_bcast(b, b.bvb, b.bias_row, extra_waits=s.wf(f0))

    def cast_in(dst_off, src, neg=False, extra=()):
        done = 0
        for c in range(8):
            sl = c % 2
            w = list(extra) if c == 0 else []
            w.append(("ve", cast_in.free[sl]))
            lr = s.put("sync", lambda e, o=b.stg_f[sl][:, :], i=src[c * 128:(c + 1) * 128, :]:
                       e.dma_start(out=o, in_=i), waits=w, inc=f"lw{sl}", amt=16)
            if neg:
                done = s.put("vector", lambda e, o=b.big[:, dst_off + c * 1024: dst_off + (c + 1) * 1024],
                             i=b.stg_f[sl][:, :]: e.tensor_scalar(o, i, -1.0, None, ALU.mult),
                             waits=[(f"lw{sl}", lr)], inc="ve")
            else:
                done = s.put("vector", lambda e, o=b.big[:, dst_off + c * 1024: dst_off + (c + 1) * 1024],
                             i=b.stg_f[sl][:, :]: e.tensor_copy(o, i),
                             waits=[(f"lw{sl}", lr)], inc="ve")
            cast_in.free[sl] = done
        return done

    cast_in.free = [b.stgf_free[0], b.stgf_free[1]]
    cast_in(SP_WR, names["s_wr"])
    cast_in(SP_WI, names["s_wi"])
    cast_in(SP_WNEG, names["s_wi"], neg=True)
    cast_in(SP_FR, names["c_fr"])
    c_all = cast_in(SP_FI, names["c_fi"])

    def big_at(off, c, lo, n):
        return b.big[:, off + c * 1024 + lo: off + c * 1024 + lo + n]

    bank_free = [0, 0]
    half = [b.psA[:, 0:512], b.psA[:, 512:1024]]
    for tgt, terms in ((SP_AT, ((SP_WR, SP_FR), (SP_WNEG, SP_FI))),
                      (None, ((SP_WI, SP_FR), (SP_WR, SP_FI)))):
        for oc in range(8):
            for ih in range(2):
                j = (oc * 2 + ih) % 2
                mmd = None
                first = True
                for (wsrc, fsrc) in terms:
                    for d in range(8):
                        w = [("ve", c_all), ("ve", bank_free[j])] if first else []
                        last = (wsrc, d) == (terms[1][0], 7)
                        kw = dict(inc="pe") if last else {}
                        mmd = s.put("tensor", lambda e, o=half[j],
                                    a=big_at(wsrc, d, oc * 128, 128),
                                    r=big_at(fsrc, d, ih * 512, 512),
                                    st=first, sp=last:
                                    e.matmul(o, a, r, start=st, stop=sp), waits=w, **kw)
                        first = False
                if tgt is not None:
                    dst = b.big[:, tgt + oc * 1024 + ih * 512: tgt + oc * 1024 + ih * 512 + 512]
                else:
                    dst = b.wbuf[:, oc * 1024 + ih * 512: oc * 1024 + ih * 512 + 512]
                bank_free[j] = s.put("vector", lambda e, o=dst, i=half[j]: e.tensor_copy(o, i),
                                     waits=[("pe", mmd)], inc="ve")
    f1 = s.fence()
    cast_in.free = [f1["ve"], f1["ve"]]
    cast_in(SP_GR, names["c_gr"], extra=[("pe", f1["pe"])])
    g_all = cast_in(SP_GIN, names["c_gin"])
    b.stgf_free[0] = max(b.stgf_free[0], cast_in.free[0])
    b.stgf_free[1] = max(b.stgf_free[1], cast_in.free[1])
    for ic in range(8):
        for jh in range(2):
            j = (ic * 2 + jh) % 2
            mmd = None
            first = True
            for (msrc, gsrc) in ((SP_AT, SP_GR), (None, SP_GIN)):
                for o_ in range(8):
                    w = [("ve", g_all), ("ve", bank_free[j])] if first else []
                    last = (gsrc == SP_GIN and o_ == 7)
                    kw = dict(inc="pe") if last else {}
                    if msrc is not None:
                        a_ap = b.big[:, SP_AT + o_ * 1024 + ic * 128: SP_AT + o_ * 1024 + ic * 128 + 128]
                    else:
                        a_ap = b.wbuf[:, o_ * 1024 + ic * 128: o_ * 1024 + ic * 128 + 128]
                    mmd = s.put("tensor", lambda e, o=half[j], a=a_ap,
                                r=big_at(gsrc, o_, jh * 512, 512), st=first, sp=last:
                                e.matmul(o, a, r, start=st, stop=sp), waits=w, **kw)
                    first = False
            dst = b.big[:, SP_WEFF + ic * 1024 + jh * 512: SP_WEFF + ic * 1024 + jh * 512 + 512]
            bank_free[j] = s.put("vector", lambda e, o=dst, i=half[j]: e.tensor_copy(o, i),
                                 waits=[("pe", mmd)], inc="ve")
    weff_ready = bank_free[0] if bank_free[0] > bank_free[1] else bank_free[1]
    if DBG_SPILL_EXT and "dbg_weff" in names:
        s.put("gpsimd", lambda e, o=names["dbg_weff"][0:128, :],
              i=b.big[:, SP_WEFF:SP_WEFF + 8192]: e.dma_start(out=o, in_=i),
              waits=[("ve", weff_ready)], inc="so", amt=16)
        s.put("gpsimd", lambda e, o=names["dbg_weff"][128:256, :],
              i=b.big[:, SP_AT:SP_AT + 8192]: e.dma_start(out=o, in_=i), inc="so", amt=16)
        s.put("gpsimd", lambda e, o=names["dbg_weff"][256:384, :],
              i=b.wbuf[:, :]: e.dma_start(out=o, in_=i), inc="so", amt=16)

    b.ln_state = {}
    xa_free = [0, 0]
    store_cnt_for_xn = [0, 0]
    pstr_free = 0
    pbank_free = [0, 0]
    prev_trv = 0
    prev_mm = 0
    prev_xnv = 0
    prev_ap = 0
    for g in range(32):
        sl = g % 2
        xr = s.put("sync", lambda e, o=b.xa[sl][:, :], i=xin[g * 128:(g + 1) * 128, :]:
                   e.dma_start(out=o, in_=i), waits=[("ve", xa_free[sl])],
                   inc=f"lx{sl}", amt=16)
        mean, rstd, sd = emit_ln(b, b.xa[sl], waits=[(f"lx{sl}", xr)])
        uv = s.put("vector", lambda e, o=b.u_f[:, :], i=b.xa[sl][:, :], m=mean, r=rstd:
                   e.tensor_scalar(o, i, m, r, ALU.subtract, ALU.mult),
                   waits=[("se", sd), ("ve", prev_xnv)], inc="ve")
        b.ln_state["rstd_read"] = uv
        uh = s.put("vector", lambda e, o=b.u_h[:, :], i=b.u_f[:, :]: e.tensor_copy(o, i),
                   waits=[("pe", prev_trv), ("ve", uv)], inc="ve")
        for c in range(8):
            w = [("ve", uh)] + ([("ve", pstr_free), ("lc", b.ident_ld)] if c == 0 else [])
            kw = dict(inc="pe") if c == 7 else {}
            tr = s.put("tensor", lambda e, o=b.pstr[:, c * 128:(c + 1) * 128],
                       i=b.u_h[:, c * 128:(c + 1) * 128], d=b.identb[:, :]:
                       e.transpose(o, i, d), waits=w, **kw)
        u2h = s.put("vector", lambda e, o=b.u2T_h[:, :], i=b.pstr[:, :]: e.tensor_copy(o, i),
                    waits=[("pe", tr), ("pe", prev_mm)], inc="ve")
        pstr_free = u2h
        prev_trv = tr
        pod = [None, None]
        for dh in range(2):
            for d in range(8):
                w = []
                if d == 0:
                    w = [("ve", u2h), ("ve", weff_ready), ("ve", pbank_free[dh])]
                kw = dict(inc="pe") if d == 7 else {}
                pod[dh] = s.put("tensor", lambda e, o=half[dh],
                                a=b.u2T_h[:, d * 128:(d + 1) * 128],
                                r=b.big[:, SP_WEFF + d * 1024 + dh * 512: SP_WEFF + d * 1024 + dh * 512 + 512],
                                st=(d == 0), sp=(d == 7):
                                e.matmul(o, a, r, start=st, stop=sp), waits=w, **kw)
        ev0 = s.put("vector", lambda e, o=b.zb[:, 0:512], i=half[0], j=b.bvb[:, 0:512]:
                    e.tensor_tensor(o, i, j, ALU.add),
                    waits=[("pe", pod[0]), ("ve", prev_ap)], inc="ve")
        ev1 = s.put("vector", lambda e, o=b.zb[:, 512:1024], i=half[1], j=b.bvb[:, 512:1024]:
                    e.tensor_tensor(o, i, j, ALU.add), waits=[("pe", pod[1])], inc="ve")
        pbank_free[0] = ev0
        pbank_free[1] = ev1
        prev_mm = pod[1]
        mean, rstd, sd = emit_ln(b, b.zb, waits=[("ve", ev1)])
        ap1 = s.put("vector", lambda e, o=b.tmpa[:, :], i=b.zb[:, 0:512], m=mean, r=rstd:
                    e.tensor_scalar(o, i, m, r, ALU.subtract, ALU.mult),
                    waits=[("se", sd), ("ve", prev_xnv)], inc="ve")
        ap2 = s.put("vector", lambda e, o=b.tmpb[:, :], i=b.zb[:, 512:1024], m=mean, r=rstd:
                    e.tensor_scalar(o, i, m, r, ALU.subtract, ALU.mult),
                    waits=[("ve", ap1)], inc="ve")
        b.ln_state["rstd_read"] = ap2
        prev_ap = ap2
        x1 = s.put("vector", lambda e, o=b.xn[sl][:, 0:512], i=b.tmpa[:, :], j=b.xa[sl][:, 0:512]:
                   e.tensor_add(o, i, j),
                   waits=[(f"sx{sl}", store_cnt_for_xn[sl]), ("ve", ap2)], inc="ve")
        xnv = s.put("vector", lambda e, o=b.xn[sl][:, 512:1024], i=b.tmpb[:, :], j=b.xa[sl][:, 512:1024]:
                    e.tensor_add(o, i, j), waits=[("ve", x1)], inc="ve")
        prev_xnv = xnv
        xa_free[sl] = xnv
        store_cnt_for_xn[sl] = s.put("gpsimd", lambda e, o=xout[g * 128:(g + 1) * 128, :],
                                     i=b.xn[sl][:, :]: e.dma_start(out=o, in_=i),
                                     waits=[("ve", xnv)], inc=f"sx{sl}", amt=16)


def build_launch(segs):
    nc = bass.Bass("TRN2", target_bir_lowering=False)
    names = {}

    ext_ins = []

    def din(name, shape, dt=F32):
        names[name] = nc.dram_tensor(name, shape, dt, kind="ExternalInput").ap()
        ext_ins.append(name)

    def dout(name, shape, dt=F32):
        names[name] = nc.dram_tensor(name, shape, dt, kind="ExternalOutput").ap()

    def dint(name, shape, dt):
        names[name] = nc.dram_tensor(name, shape, dt, kind="Internal").ap()

    din("c_identb", [128, 128], BF16)
    needs_identf = any(s == "moe" for s in segs)
    if needs_identf:
        din("c_identf", [128, 128], F32)
    has_stream = any(s0 in ("attn0", "attn1", "moe", "spec") for s0 in segs)
    if has_stream:
        din("x_in", [S, D])
        dout("y", [S, D])
    for sg in segs:
        if sg in ("attn0", "attn1"):
            L = 0 if sg == "attn0" else 1
            for w in ("wqT", "wkT", "wvT", "woT"):
                din(f"a{L}_{w}", [D, D])
            din(f"a{L}_bqc", [128, 8])
            din(f"a{L}_bkc", [128, 8])
            din(f"a{L}_bv", [1, D])
            din(f"a{L}_bo", [1, D])
            if L == 0:
                din("rcos", [S, 512])
                din("rsin", [S, 512])
            if "qspill" not in names:
                if DBG_SPILL_EXT:
                    dout("qspill", [D, S], BF16)
                    dout("kspill", [D, S], BF16)
                    dout("dbg_rxq", [512, 8192], BF16)
                    dout("dbg_uh", [S, D], BF16)
                    dout("dbg_uf", [S, D], F32)
                    dout("dbg_ag", [S, 4], F32)
                    dout("dbg_att", [128, 4096], BF16)
                    dout("dbg_vaug", [128, 33280], BF16)
                    dout("dbg_wk", [128, 8192], BF16)
                    dout("dbg_wv", [128, 8192], BF16)
                    dout("dbg_wq", [128, 8192], BF16)
                else:
                    dint("qspill", [D, S], BF16)
                    dint("kspill", [D, S], BF16)
        elif sg == "mbuild":
            din("mb_win", [4096, 1024])
            din("mb_woutT", [4096, 1024])
            dout("m_out", [1024, 1024])
        elif sg == "mbuild8":
            din("m_win8", [8, 4096, 1024])
            din("m_woutT8", [8, 4096, 1024])
            dint("m_spill", [8, 1024, 1024], F32)
        elif sg == "moe":
            din("g_wTc", [128, 64])
            if "mbuild8" not in segs:
                din("m_all", [8, 1024, 1024])
            if DBG_SPILL_EXT:
                dout("dbg_gat", [S, 48])
        elif sg == "spec":
            if DBG_SPILL_EXT:
                dout("dbg_weff", [384, 8192], BF16)
            din("s_wr", [D, D])
            din("s_wi", [D, D])
            din("s_b1", [1, D])
            din("c_fr", [D, D])
            din("c_fi", [D, D])
            din("c_gr", [D, D])
            din("c_gin", [D, D])

    sch = Sch(DMA_SEMS + ["pe", "ve", "se"])
    with ExitStack() as ctx:
        b = alloc_tensors(nc, ctx)
        b.sch = sch
        b.ln_state = {}
        sems = {}
        for s0 in DMA_SEMS + ["pe", "ve", "se"]:
            sems[s0] = ctx.enter_context(nc.semaphore(s0))
        sch.sems = sems
        emit_consts(b, names)
        xi, xo = (names.get("x_in"), names.get("y"))
        stream_segs = [s0 for s0 in segs if s0 not in ("mbuild", "mbuild8")]
        cur_in = xi

        def seg_fence():
            f = sch.fence()
            for e0 in ENGINES:
                sch.put(e0, lambda e: None, waits=sch.wf(f))
            b.wr_fence = f["pe"]

        for i, sg in enumerate(segs):
            if i > 0:
                seg_fence()
            if sg == "mbuild":
                emit_mbuild(b, names,
                            lambda hk, wout=False, a=names["mb_win"], bb=names["mb_woutT"]:
                                (bb if wout else a)[hk * 128:(hk + 1) * 128, :],
                            None,
                            lambda ch, dh, t=names["m_out"]: t[ch * 128:(ch + 1) * 128, dh * 512:(dh + 1) * 512])
                continue
            if sg == "mbuild8":
                for e_ in range(8):
                    emit_mbuild(b, names,
                                lambda hk, wout=False, E=e_, a=names["m_win8"], bb=names["m_woutT8"]:
                                    (bb if wout else a)[E, hk * 128:(hk + 1) * 128, :],
                                None,
                                lambda ch, dh, E=e_, t=names["m_spill"]: t[E, ch * 128:(ch + 1) * 128, dh * 512:(dh + 1) * 512])
                continue
            last = (sg == stream_segs[-1])
            if last:
                cur_out = xo
            else:
                nm_mid = f"x_mid{i}"
                names[nm_mid] = nc.dram_tensor(nm_mid, [S, D], F32, kind="Internal").ap()
                cur_out = names[nm_mid]
            if sg == "attn0":
                emit_attn(b, names, 0, cur_in, cur_out)
            elif sg == "attn1":
                emit_attn(b, names, 1, cur_in, cur_out)
            elif sg == "moe":
                emit_moe(b, names, cur_in, cur_out,
                         m_src=names.get("m_spill", names.get("m_all")))
            elif sg == "spec":
                emit_spec(b, names, cur_in, cur_out)
            cur_in = cur_out
        fin = sch.fence()
        sch.put("gpsimd", lambda e: None,
                waits=[(s0, fin[s0]) for s0 in ("sk0", "sk1", "sx0", "sx1", "so")])
        block = ctx.enter_context(nc.Block())
        block.sync(lambda e: sch.run("sync", e))
        block.tensor(lambda e: sch.run("tensor", e))
        block.vector(lambda e: sch.run("vector", e))
        block.scalar(lambda e: sch.run("scalar", e))
        block.gpsimd(lambda e: sch.run("gpsimd", e))
    return nc, names, ext_ins



def _host_prep(inputs):
    import ml_dtypes
    c = {}
    f32 = np.float32
    asc = np.ascontiguousarray
    for L, p in ((0, "L0_attn_"), (1, "L1_attn_")):
        c[f"a{L}_wqT"] = asc(np.asarray(inputs[p + "Wq"], f32).T)
        c[f"a{L}_wkT"] = asc(np.asarray(inputs[p + "Wk"], f32).T)
        c[f"a{L}_wvT"] = asc(np.asarray(inputs[p + "Wv"], f32).T)
        c[f"a{L}_woT"] = asc(np.asarray(inputs[p + "Wo"], f32).T)
        c[f"a{L}_bqc"] = asc(np.asarray(inputs[p + "bq"], f32).reshape(8, 128).T)
        c[f"a{L}_bkc"] = asc(np.asarray(inputs[p + "bk"], f32).reshape(8, 128).T)
        c[f"a{L}_bv"] = asc(np.asarray(inputs[p + "bv"], f32).reshape(1, D))
        c[f"a{L}_bo"] = asc(np.asarray(inputs[p + "bo"], f32).reshape(1, D))
    freq = np.exp(np.arange(0, D, 2, dtype=np.float64) * (-np.log(10000.0) / D))
    ang = np.arange(S, dtype=np.float64)[:, None] * freq[None, :]
    c["rcos"] = np.cos(ang).astype(f32)
    c["rsin"] = np.sin(ang).astype(f32)
    c["g_wTc"] = asc(np.asarray(inputs["L0_ffn_w_gate"], f32).T.reshape(8, 128, 8)
                     .transpose(1, 0, 2).reshape(128, 64))
    c["m_win"] = np.asarray(inputs["L0_ffn_w_in"], f32)
    c["m_woutT"] = asc(np.asarray(inputs["L0_ffn_w_out"], f32).transpose(0, 2, 1))
    c["s_wr"] = np.asarray(inputs["L1_ffn_w_freq_real"], f32)
    c["s_wi"] = np.asarray(inputs["L1_ffn_w_freq_imag"], f32)
    c["s_b1"] = asc(np.asarray(inputs["L1_ffn_bias"], f32).reshape(1, D))
    dd = np.arange(D, dtype=np.float64)
    th = 2.0 * np.pi * np.outer(dd, dd) / D
    c["c_fr"] = np.cos(th).astype(f32)
    c["c_fi"] = (-np.sin(th)).astype(f32)
    c["c_gr"] = (np.cos(th) / D).astype(f32)
    c["c_gin"] = (-np.sin(th) / D).astype(f32)
    c["c_identb"] = np.eye(128, dtype=ml_dtypes.bfloat16)
    c["c_identf"] = np.eye(128, dtype=f32)
    c["x"] = np.asarray(inputs["L0_attn_x"], f32)
    return c


def _in_maps_for(ext_ins, c, core, xcur=None, m_all=None):
    m = {}
    for k in ext_ins:
        if k == "x_in":
            m[k] = xcur[core]
        elif k == "mb_win":
            m[k] = c["m_win"][core]
        elif k == "mb_woutT":
            m[k] = c["m_woutT"][core]
        elif k == "m_win8":
            m[k] = c["m_win"]
        elif k == "m_woutT8":
            m[k] = c["m_woutT"]
        elif k == "m_all":
            m[k] = m_all
        else:
            m[k] = c[k]
    return m


FUSED = True


def kernel(**inputs):
    c = _host_prep(inputs)
    xcur = c["x"]

    def launch(segs, xcur=None, m_all=None, outkey="y"):
        nc, names, ext_ins = build_launch(segs)
        maps = [_in_maps_for(ext_ins, c, core, xcur=xcur, m_all=m_all) for core in range(NCORES)]
        res = run_bass_kernel_spmd(nc, maps, core_ids=list(range(NCORES)))
        return [np.asarray(r[outkey]) for r in res.results]

    if FUSED:
        y = launch(["attn0", "mbuild8", "moe", "attn1", "spec"], xcur=xcur)
        return np.stack(y, axis=0)

    x1 = launch(["attn0"], xcur=xcur)
    m_outs = launch(["mbuild"], xcur=None, outkey="m_out")
    m_all = np.stack(m_outs, axis=0)
    x1 = np.stack(x1, axis=0)
    x2 = np.stack(launch(["moe"], xcur=x1, m_all=m_all), axis=0)
    x3 = np.stack(launch(["attn1"], xcur=x2), axis=0)
    y = np.stack(launch(["spec"], xcur=x3), axis=0)
    return y
